# version 11
# baseline (speedup 1.0000x reference)
import numpy as np
import ml_dtypes
from contextlib import ExitStack

import concourse.bass as bass
import concourse.mybir as mybir
from concourse.bass_utils import run_bass_kernel_spmd

F32 = mybir.dt.float32
BF16 = mybir.dt.bfloat16
OP = mybir.AluOpType
AF = mybir.ActivationFunctionType
BF = ml_dtypes.bfloat16

B, S, D = 4, 4096, 1024
T = 2048
N = 16
E, F = 64, 512
DC = 8
LC = 512
NLC = (2 * T) // LC
TC = 256
NTC = T // TC
EPS = 1e-5


class Sem:
    def __init__(self, nc, es, name):
        self.h = es.enter_context(nc.semaphore(name))
        self.v = 0

    def inc(self, instr, k=1):
        instr.then_inc(self.h, k)
        self.v += k
        return instr


def build_nc(_upto=99, _dbg=False):
    nc = bass.Bass("TRN2", target_bir_lowering=False, detect_race_conditions=False)
    g, v, t, s, sy = nc.gpsimd, nc.vector, nc.tensor, nc.scalar, nc.sync

    dI = lambda n_, sh: nc.dram_tensor(n_, sh, F32, kind="ExternalInput")
    x_own = dI("x_own", [T, D])
    x_pre = dI("x_pre", [T, D])
    alpha = dI("alpha", [128, 1])
    pjW = dI("pjW", [D, D])
    fcW = dI("fcW", [D, D])
    bcW = dI("bcW", [D, D])
    dbW = dI("dbW", [128, DC * 128])
    dtW = dI("dtW", [64, D])
    Alog = dI("Alog", [128, DC * N])
    Dv = dI("Dv", [128, DC])
    lnw = dI("lnw", [128, DC]); lnb = dI("lnb", [128, DC])
    pjb = dI("pjb", [128, DC]); fcb = dI("fcb", [128, DC]); bcb = dI("bcb", [128, DC]); dtb = dI("dtb", [128, DC])
    ctW = dI("ctW", [128, DC * E])
    ebv = dI("ebv", [E])
    wsu = dI("wsu", [D, F])
    wsd = dI("wsd", [F, D])
    wup = dI("wup", [E, D, F])
    wdn = dI("wdn", [E, F, D])
    out = nc.dram_tensor("out", [T, D], F32, kind="ExternalOutput")

    _ik = "ExternalOutput" if _dbg else "Internal"
    xn2d = nc.dram_tensor("xn2d", [D, 2 * T], BF16, kind=_ik)
    xnd = nc.dram_tensor("xnd", [D, T], BF16, kind=_ik)
    xpd = nc.dram_tensor("xpd", [D, 2 * T], BF16, kind=_ik)
    rtd = nc.dram_tensor("rtd", [D, T], F32, kind=_ik)
    sd = nc.dram_tensor("sd", [128, DC * T], F32, kind=_ik)
    cwfd = nc.dram_tensor("cwfd", [64, T], BF16, kind="Internal")
    if _dbg:
        dbg_sacc = nc.dram_tensor("dbg_sacc", [128, DC * T], F32, kind="ExternalOutput")
        dbg_cw = nc.dram_tensor("dbg_cw", [64, T], F32, kind="ExternalOutput")
        dbg_hst = nc.dram_tensor("dbg_hst", [128, 128], F32, kind="ExternalOutput")
        dbg_st = nc.dram_tensor("dbg_st", [128, 8 * 32], F32, kind="ExternalOutput")
        dbg_st2 = nc.dram_tensor("dbg_st2", [128, 8 * 32], F32, kind="ExternalOutput")
        dbg_lnw = nc.dram_tensor("dbg_lnw", [128, DC], F32, kind="ExternalOutput")

    identI = nc.inline_tensor(np.eye(128, dtype=np.float32), "identI")
    ohbc_np = np.zeros((128, 16 * 128), dtype=np.float32)
    for s_ in range(32):
        base, colb = (s_ % 2) * 32, (s_ // 2) * 128
        ohbc_np[base + s_, colb:colb + 128] = 1.0
    ohbcI = nc.inline_tensor(ohbc_np, "ohbcI")
    ohe_np = np.zeros((64, 64 * 128), dtype=BF)
    for e_ in range(64):
        ohe_np[e_, e_ * 128:(e_ + 1) * 128] = 1.0
    oheI = nc.inline_tensor(ohe_np, "oheI")

    es0 = ExitStack()
    sb = lambda n_, sh, dt_: es0.enter_context(nc.sbuf_tensor(n_, sh, dt_))

    A_sb = sb("A_sb", [128, DC * N], F32)
    Dv_sb = sb("Dv_sb", [128, DC], F32)
    lnw_sb = sb("lnw_sb", [128, DC], F32)
    lnb_sb = sb("lnb_sb", [128, DC], F32)
    pjb_sb = sb("pjb_sb", [128, DC], F32)
    fcb_sb = sb("fcb_sb", [128, DC], F32)
    bcb_sb = sb("bcb_sb", [128, DC], F32)
    dtb_sb = sb("dtb_sb", [128, DC], F32)
    al_sb = sb("al_sb", [128, 1], F32)
    id_sb = sb("id_sb", [128, 128], F32)
    hst_sb = sb("hst_sb", [128, 128], F32)
    dbc_sb = sb("dbc_sb", [128, LC], F32)
    dbWsb = sb("dbWsb", [128, DC * 128], F32)
    dtWsb = sb("dtWsb", [128, D], F32)
    st_sb = sb("st_sb", [128, 8 * 32], F32)
    eps_sb = sb("eps_sb", [128, 1], F32)

    sT = Sem(nc, es0, "sT"); sV = Sem(nc, es0, "sV"); sS = Sem(nc, es0, "sS")
    dC = Sem(nc, es0, "dC")
    dCg = Sem(nc, es0, "dCg")
    dX = [Sem(nc, es0, f"dX{i}") for i in range(2)]
    dSp = [Sem(nc, es0, f"dSp{i}") for i in range(2)]
    dW = [Sem(nc, es0, f"dW{i}") for i in range(4)]
    dR = [Sem(nc, es0, f"dR{i}") for i in range(2)]
    dOp = [Sem(nc, es0, f"dOp{i}") for i in range(2)]
    dSd = Sem(nc, es0, "dSd")
    dGx = [Sem(nc, es0, f"dGx{i}") for i in range(2)]
    dSg = [Sem(nc, es0, f"dSg{i}") for i in range(2)]

    def dma(eng, sem, out_, in_, self_wait=True, **kw):
        if self_wait and sem.v > 0:
            eng.wait_ge(sem.h, sem.v)
        eng.dma_start(out=out_, in_=in_, **kw).then_inc(sem.h, 16)
        sem.v += 16
        return sem.v

    dma(sy, dC, A_sb[:, :], Alog.ap()[:, :], self_wait=False)
    for dram, tile in [(Dv, Dv_sb), (lnw, lnw_sb), (lnb, lnb_sb), (pjb, pjb_sb),
                       (fcb, fcb_sb), (bcb, bcb_sb), (dtb, dtb_sb)]:
        dma(sy, dC, tile[:, :], dram.ap()[:, :], self_wait=False)
    dma(sy, dC, al_sb[:, :], alpha.ap()[:, :], self_wait=False)
    dma(sy, dC, dbWsb[:, :], dbW.ap()[:, :], self_wait=False)
    dma(sy, dC, dtWsb[64:128, :], dtW.ap()[:, :], self_wait=False)
    dma(sy, dC, id_sb[:, :], identI.ap()[:, :], self_wait=False)
    P0_LOADS = dC.v
    s.wait_ge(dC.h, P0_LOADS)
    sS.inc(s.activation(A_sb[:, :], A_sb[:, :], AF.Exp))
    v.wait_ge(sS.h, sS.v)
    v.tensor_single_scalar(A_sb[:, :], A_sb[:, :], -1.0, OP.mult)
    v.memset(dbc_sb[:, :], 0.0)
    v.memset(eps_sb[:, :], EPS)
    sV.inc(v.engine_nop())
    V_INIT = sV.v

    stv = st_sb[:, :].rearrange("p (k c) -> p k c", k=8)
    ST_S, ST_Q, ST_MU, ST_V, ST_R, ST_RR2, ST_T1, ST_T2 = range(8)
    stc = lambda k, ti: st_sb[:, k * 32 + ti: k * 32 + ti + 1]

    xsrc = lambda ti: (x_pre if ti < 16 else x_own).ap()[(ti % 16) * 128:(ti % 16) * 128 + 128, :]
    sd3 = sd.ap()[:, :].rearrange("p (dc t) -> p dc t", dc=DC)
    with ExitStack() as ph:
        psb = lambda n_, sh, dt_: ph.enter_context(nc.sbuf_tensor(n_, sh, dt_))
        pps = lambda n_, sh: ph.enter_context(nc.psum_tensor(n_, sh, F32))
        xbuf = [psb(f"xbuf{i}", [128, D], F32) for i in range(2)]
        sq_scr = psb("sq_scr", [128, D], F32)
        xn_tm = psb("xn_tm", [128, D], F32)
        xr_tm = psb("xr_tm", [128, D], F32)
        xn2b = [psb(f"xn2b{i}", [128, D], BF16) for i in range(2)]
        xnbb = [psb(f"xnbb{i}", [128, D], BF16) for i in range(2)]
        sstg = [psb(f"sstg{i}", [128, D], F32) for i in range(2)]
        trA = [pps(f"trA{i}", [128, D]) for i in range(2)]
        trB = [pps(f"trB{i}", [128, D]) for i in range(2)]

        v_tile, s_tile = [], []
        for ti in range(32):
            par = ti % 2
            if ti >= 2:
                g.wait_ge(sV.h, v_tile[ti - 2])
                g.wait_ge(sS.h, s_tile[ti - 2])
            XLD = dma(g, dGx[par], xbuf[par][:, :], xsrc(ti))
            v.wait_ge(dGx[par].h, XLD)
            if ti == 0:
                v.wait_ge(sV.h, V_INIT)
            sV.inc(v.reduce_sum(stc(ST_S, ti), xbuf[par][:, :], axis=mybir.AxisListType.X))
            v_tile.append(sV.v)
            s.wait_ge(dGx[par].h, XLD)
            sS.inc(s.activation(sq_scr[:, :], xbuf[par][:, :], AF.Square, accum_out=stc(ST_Q, ti)))
            s_tile.append(sS.v)
        v.wait_ge(sS.h, sS.v)
        v.tensor_single_scalar(stv[:, ST_MU, :], stv[:, ST_S, :], 1.0 / D, OP.mult)
        v.tensor_single_scalar(stv[:, ST_T1, :], stv[:, ST_Q, :], 1.0 / D, OP.mult)
        v.tensor_mul(stv[:, ST_T2, :], stv[:, ST_MU, :], stv[:, ST_MU, :])
        sV.inc(v.tensor_sub(stv[:, ST_V, :], stv[:, ST_T1, :], stv[:, ST_T2, :]))
        s.wait_ge(sV.h, sV.v)
        sS.inc(s.activation(stv[:, ST_T1, :], stv[:, ST_V, :], AF.Sqrt, bias=eps_sb[:, 0:1]))
        v.wait_ge(sS.h, sS.v)
        sV.inc(v.reciprocal(stv[:, ST_R, :], stv[:, ST_T1, :]))
        v.wait_ge(sV.h, sV.v)
        v.tensor_mul(stv[:, ST_T1, :], stv[:, ST_V, :], stv[:, ST_R, :])
        sV.inc(v.tensor_mul(stv[:, ST_T2, :], stv[:, ST_T1, :], stv[:, ST_R, :]))
        s.wait_ge(sV.h, sV.v)
        sS.inc(s.activation(stv[:, ST_T1, :], stv[:, ST_T2, :], AF.Sqrt, bias=eps_sb[:, 0:1]))
        v.wait_ge(sS.h, sS.v)
        sV.inc(v.reciprocal(stv[:, ST_T2, :], stv[:, ST_T1, :]))
        v.wait_ge(sV.h, sV.v)
        sV.inc(v.tensor_mul(stv[:, ST_RR2, :], stv[:, ST_R, :], stv[:, ST_T2, :]))
        if _dbg:
            sy.wait_ge(sV.h, sV.v)
            dma(sy, dOp[1], dbg_st2.ap()[:, :], st_sb[:, :], self_wait=False)
            sy.wait_ge(dOp[1].h, dOp[1].v)

        xn2d3 = xn2d.ap()[:, :].rearrange("(dc p) t -> p dc t", p=128)
        xnd3 = xnd.ap()[:, :].rearrange("(dc p) t -> p dc t", p=128)
        ev_v, ev_s, sp_done, sg_done, trd = [], [], [], [], []
        for ti in range(32):
            own = ti >= 16
            par = ti % 2
            if ti >= 2:
                sy.wait_ge(sV.h, ev_v[ti - 2])
            else:
                sy.wait_ge(sV.h, v_tile[-1])
                sy.wait_ge(sS.h, s_tile[-1])
            XLD = dma(sy, dX[par], xbuf[par][:, :], xsrc(ti))
            v.wait_ge(dX[par].h, XLD)
            if ti >= 1:
                v.wait_ge(sT.h, trd[ti - 1])
            v.tensor_scalar(xn_tm[:, :], xbuf[par][:, :], stc(ST_MU, ti), stc(ST_R, ti), OP.subtract, OP.mult)
            sV.inc(v.tensor_single_scalar(xr_tm[:, :], xn_tm[:, :], stc(ST_T2, ti), OP.mult))
            NORM_V = sV.v
            t.wait_ge(sV.h, NORM_V)
            if ti >= 2:
                t.wait_ge(sS.h, ev_s[ti - 2])
                t.wait_ge(sV.h, ev_v[ti - 2])
            _ti = None
            for dc in range(DC):
                _ti = t.transpose(trA[par][:, dc * 128:(dc + 1) * 128], xr_tm[:, dc * 128:(dc + 1) * 128], id_sb[:, :])
            if own:
                for dc in range(DC):
                    _ti = t.transpose(trB[par][:, dc * 128:(dc + 1) * 128], xn_tm[:, dc * 128:(dc + 1) * 128], id_sb[:, :])
            sT.inc(_ti); trd.append(sT.v)
            s.wait_ge(sT.h, sT.v)
            if ti >= 2:
                s.wait_ge(dSp[par].h, sp_done[ti - 2])
                s.wait_ge(dSg[par].h, sg_done[ti - 2])
            _si = None
            for dc in range(DC):
                _si = s.activation(xn2b[par][:, dc * 128:(dc + 1) * 128], trA[par][:, dc * 128:(dc + 1) * 128],
                                   AF.Identity, bias=lnb_sb[:, dc:dc + 1], scale=lnw_sb[:, dc:dc + 1])
            if own:
                _si = s.copy(xnbb[par][:, :], trB[par][:, :])
            sS.inc(_si); ev_s.append(sS.v)
            v.wait_ge(sT.h, sT.v)
            if own:
                if len(sg_done) >= 2:
                    v.wait_ge(dSg[par].h, sg_done[-2])
                sV.inc(v.tensor_copy(sstg[par][:, :], trB[par][:, :]))
            else:
                sV.inc(v.engine_nop())
            ev_v.append(sV.v)
            sy.wait_ge(sS.h, ev_s[-1])
            if dSp[par].v > 0:
                sy.wait_ge(dSp[par].h, dSp[par].v)
            dma(sy, dSp[par], xn2d3[:, :, ti * 128:(ti + 1) * 128],
                xn2b[par][:, :].rearrange("p (dc t) -> p dc t", dc=DC), self_wait=False)
            sp_done.append(dSp[par].v)
            if own:
                ti2 = ti - 16
                g.wait_ge(sS.h, ev_s[-1])
                g.wait_ge(sV.h, ev_v[-1])
                if dSg[par].v > 0:
                    g.wait_ge(dSg[par].h, dSg[par].v)
                dma(g, dSg[par], xnd3[:, :, ti2 * 128:(ti2 + 1) * 128],
                    xnbb[par][:, :].rearrange("p (dc t) -> p dc t", dc=DC), self_wait=False)
                dma(g, dSg[par], sd3[:, :, ti2 * 128:(ti2 + 1) * 128],
                    sstg[par][:, :].rearrange("p (dc t) -> p dc t", dc=DC), self_wait=False)
            sg_done.append(dSg[par].v)
        sy.wait_ge(dSp[0].h, dSp[0].v)
        sy.wait_ge(dSp[1].h, dSp[1].v)
        sy.wait_ge(dSg[0].h, dSg[0].v)
        sy.wait_ge(dSg[1].h, dSg[1].v)
        if _dbg:
            dma(sy, dOp[0], dbg_st.ap()[:, :], st_sb[:, :], self_wait=False)
            dma(sy, dOp[0], dbg_lnw.ap()[:, :], lnw_sb[:, :], self_wait=False)
            sy.wait_ge(dOp[0].h, dOp[0].v)
    nc.all_engine_barrier()
    if _dbg and _upto <= 1:
        es0.close(); return nc

    xpd3 = xpd.ap()[:, :].rearrange("(dc p) t -> p dc t", p=128)
    xn2d3r = xn2d.ap()[:, :].rearrange("(ki p) t -> p ki t", p=128)
    with ExitStack() as ph:
        psb = lambda n_, sh, dt_: ph.enter_context(nc.sbuf_tensor(n_, sh, dt_))
        wmat = psb("wmat", [128, DC * D], BF16)
        x2rd = [psb(f"x2rd{i}", [128, DC * LC], BF16) for i in range(2)]
        xpb = [psb(f"xpb{i}", [128, LC], BF16) for i in range(2)]
        pj = [ph.enter_context(nc.psum_tensor(f"pj{i}", [128, LC], F32)) for i in range(2)]
        if dCg.v > 0:
            g.wait_ge(dCg.h, dCg.v)
        for ki in range(DC):
            dma(g, dCg, wmat[:, ki * D:(ki + 1) * D], pjW.ap()[ki * 128:(ki + 1) * 128, :], self_wait=False)
        W_DONE = dCg.v
        ev, st_cnt, lc_mm = [], [], []
        for lc in range(NLC):
            par = lc % 2
            if lc >= 2:
                g.wait_ge(sT.h, lc_mm[lc - 2])
            XLD = dma(g, dGx[par], x2rd[par][:, :].rearrange("p (ki t) -> p ki t", ki=DC),
                      xn2d3r[:, :, lc * LC:(lc + 1) * LC])
            t.wait_ge(dGx[par].h, XLD)
            t.wait_ge(dCg.h, W_DONE)
            for mi in range(DC):
                mp = mi % 2
                if len(ev) >= 2:
                    t.wait_ge(sS.h, ev[-2])
                _mi = None
                for ki in range(DC):
                    _mi = t.matmul(pj[mp][:, :], wmat[:, ki * D + mi * 128: ki * D + (mi + 1) * 128],
                                   x2rd[par][:, ki * LC:(ki + 1) * LC], start=(ki == 0), stop=(ki == DC - 1))
                sT.inc(_mi)
                s.wait_ge(sT.h, sT.v)
                if len(st_cnt) >= 2:
                    s.wait_ge(dSp[mp].h, st_cnt[-2])
                sS.inc(s.activation(xpb[mp][:, :], pj[mp][:, :], AF.Identity, bias=pjb_sb[:, mi:mi + 1]))
                ev.append(sS.v)
                sy.wait_ge(sS.h, sS.v)
                dma(sy, dSp[mp], xpd3[:, mi, lc * LC:(lc + 1) * LC], xpb[mp][:, :])
                st_cnt.append(dSp[mp].v)
            lc_mm.append(sT.v)
        sy.wait_ge(dSp[0].h, dSp[0].v)
        sy.wait_ge(dSp[1].h, dSp[1].v)
    nc.all_engine_barrier()
    if _dbg and _upto <= 2:
        es0.close(); return nc

    with ExitStack() as ph:
        psb = lambda n_, sh, dt_: ph.enter_context(nc.sbuf_tensor(n_, sh, dt_))
        wmat = psb("wmatB", [128, DC * D], BF16)
        xprd = psb("xprd", [128, DC * LC], BF16)
        u_v = psb("u_v", [128, DC * LC], F32)
        d_v = psb("d_v", [128, DC * LC], F32)
        w_v = psb("w_v", [128, DC * LC], F32)
        y_v = psb("y_v", [128, DC * LC], F32)
        ohbc_sb = psb("ohbc_sb", [128, 16 * 128], F32)
        sp_scr = psb("sp_scr", [128, LC], F32)
        z_t = [sp_scr, sp_scr]
        dA_t = [psb(f"dA{i}", [128, LC], F32) for i in range(12)]
        HBg = psb("HBg", [128, 4 * LC], F32)
        Brep = [psb(f"Brep{i}", [128, 4 * LC], F32) for i in range(2)]
        Crep = [psb(f"Crep{i}", [128, 4 * LC], F32) for i in range(2)]
        ytmp = psb("ytmp", [128, 4 * LC], F32)
        s_tmp = psb("s_tmp", [128, DC * LC], F32)
        uP = [ph.enter_context(nc.psum_tensor(f"uP{i}", [128, LC], F32)) for i in range(2)]
        dbP = ph.enter_context(nc.psum_tensor("dbP", [128, LC], F32))
        bcP = [ph.enter_context(nc.psum_tensor(f"bcP{i}", [128, 2 * LC], F32)) for i in range(2)]

        if dCg.v > 0:
            g.wait_ge(dCg.h, dCg.v)
        dma(g, dCg, ohbc_sb[:, :], ohbcI.ap()[:, :], self_wait=False)
        xpd3r = xpd.ap()[:, :].rearrange("(dc p) t -> p dc t", p=128)
        st3d = s_tmp[:, :].rearrange("p (dc t) -> p dc t", dc=DC)
        SD_PREV = [0]

        for br, (Wd, bvec) in enumerate([(fcW, fcb_sb), (bcW, bcb_sb)]):
            g.wait_ge(sT.h, sT.v)
            g.wait_ge(dCg.h, dCg.v)
            for ki in range(DC):
                dma(g, dCg, wmat[:, ki * D:(ki + 1) * D], Wd.ap()[ki * 128:(ki + 1) * 128, :], self_wait=False)
            WBR = dCg.v
            for lc in range(NLC):
                own = lc >= NLC // 2
                sy.wait_ge(sV.h, sV.v)
                sy.wait_ge(sS.h, sS.v)
                XP_RD = dma(sy, dX[0], xprd[:, :].rearrange("p (ki t) -> p ki t", ki=DC),
                            xpd3r[:, :, lc * LC:(lc + 1) * LC])
                t.wait_ge(dX[0].h, XP_RD)
                t.wait_ge(dCg.h, WBR)
                s.wait_ge(sV.h, sV.v)
                u_evac = []
                for mi in range(DC):
                    mp = mi % 2
                    if mi >= 2:
                        t.wait_ge(sS.h, u_evac[mi - 2])
                    _mi = None
                    for ki in range(DC):
                        _mi = t.matmul(uP[mp][:, :], wmat[:, ki * D + mi * 128: ki * D + (mi + 1) * 128],
                                       xprd[:, ki * LC:(ki + 1) * LC], start=(ki == 0), stop=(ki == DC - 1))
                    sT.inc(_mi)
                    s.wait_ge(sT.h, sT.v)
                    s.activation(sp_scr[:, :], uP[mp][:, :], AF.Exp, bias=bvec[:, mi:mi + 1])
                    sS.inc(s.activation(u_v[:, mi * LC:(mi + 1) * LC], sp_scr[:, :], AF.Ln, bias=1.0))
                    u_evac.append(sS.v)
                t.wait_ge(sS.h, u_evac[-1])
                _mi = None
                for ki in range(DC):
                    _mi = t.matmul(dbP[:, :], dbWsb[:, ki * 128:(ki + 1) * 128],
                                   u_v[:, ki * LC:(ki + 1) * LC], start=(ki == 0), stop=(ki == DC - 1))
                sT.inc(_mi)
                v.wait_ge(sT.h, sT.v)
                sV.inc(v.tensor_copy(dbc_sb[:, :], dbP[:, :]))
                DBC_V = sV.v
                d_evac = []
                for dc in range(DC):
                    mp = dc % 2
                    t.wait_ge(sV.h, DBC_V)
                    if dc >= 2:
                        t.wait_ge(sS.h, d_evac[dc - 2])
                    sT.inc(t.matmul(uP[mp][:, :], dtWsb[64:128, dc * 128:(dc + 1) * 128],
                                    dbc_sb[64:128, :], start=True, stop=True))
                    s.wait_ge(sT.h, sT.v)
                    s.activation(sp_scr[:, :], uP[mp][:, :], AF.Exp, bias=dtb_sb[:, dc:dc + 1])
                    sS.inc(s.activation(d_v[:, dc * LC:(dc + 1) * LC], sp_scr[:, :], AF.Ln, bias=1.0))
                    d_evac.append(sS.v)
                v.wait_ge(sS.h, d_evac[-1])
                sV.inc(v.tensor_mul(w_v[:, :], d_v[:, :], u_v[:, :]))

                SCAN_V, DA_S, BC_T, REP_S = {}, {}, {}, {}

                def emit_bcast(n_):
                    if n_ >= 2:
                        t.wait_ge(sS.h, REP_S[n_ - 2])
                    _m2 = t.matmul(bcP[n_ % 2][:, 0:LC], ohbc_sb[((n_) % 2) * 32:((n_) % 2) * 32 + 32, ((n_) // 2) * 128:((n_) // 2) * 128 + 128],
                                   dbc_sb[((n_) % 2) * 32:((n_) % 2) * 32 + 32, :], start=True, stop=True)
                    if own:
                        _m2 = t.matmul(bcP[n_ % 2][:, LC:2 * LC], ohbc_sb[((16 + n_) % 2) * 32:((16 + n_) % 2) * 32 + 32, ((16 + n_) // 2) * 128:((16 + n_) // 2) * 128 + 128],
                                       dbc_sb[((16 + n_) % 2) * 32:((16 + n_) % 2) * 32 + 32, :], start=True, stop=True)
                    sT.inc(_m2); BC_T[n_] = sT.v

                def emit_rep(n_):
                    if n_ >= 2:
                        s.wait_ge(sV.h, SCAN_V[n_ - 2])
                    s.wait_ge(sT.h, BC_T[n_])
                    br_ = Brep[n_ % 2]
                    s.copy(br_[:, 0:LC], bcP[n_ % 2][:, 0:LC])
                    s.copy(br_[:, LC:2 * LC], br_[:, 0:LC])
                    _s3 = s.copy(br_[:, 2 * LC:4 * LC], br_[:, 0:2 * LC])
                    if own:
                        cr_ = Crep[n_ % 2]
                        s.copy(cr_[:, 0:LC], bcP[n_ % 2][:, LC:2 * LC])
                        s.copy(cr_[:, LC:2 * LC], cr_[:, 0:LC])
                        _s3 = s.copy(cr_[:, 2 * LC:4 * LC], cr_[:, 0:2 * LC])
                    sS.inc(_s3); REP_S[n_] = sS.v

                def emit_dA_half(n_, half):
                    _s2 = None
                    for dc in range(4 * half, 4 * half + 4):
                        q = n_ * 8 + dc
                        if q >= 12:
                            gg = (q - 12) // 8
                            s.wait_ge(sV.h, SCAN_V[gg])
                        _s2 = s.activation(dA_t[q % 12][:, :], d_v[:, dc * LC:(dc + 1) * LC], AF.Exp,
                                           scale=A_sb[:, dc * N + n_: dc * N + n_ + 1])
                    sS.inc(_s2); DA_S[(n_, half)] = sS.v

                hst3 = hst_sb[:, :].rearrange("p (dc n) -> p dc n", n=N)
                HBg3 = HBg[:, :].rearrange("p (j l) -> p j l", l=LC)

                def emit_scan(n_):
                    for half in range(2):
                        d0 = 4 * half
                        if half == 0:
                            v.wait_ge(sS.h, REP_S[n_])
                            v.wait_ge(sS.h, DA_S[(n_, 0)])
                        else:
                            v.wait_ge(sS.h, DA_S[(n_, 1)])
                        v.tensor_tensor(HBg[:, :], w_v[:, d0 * LC:(d0 + 4) * LC],
                                        Brep[n_ % 2][:, :], OP.mult)
                        for j in range(4):
                            dc = d0 + j
                            hb = HBg[:, j * LC:(j + 1) * LC]
                            init = 0.0 if lc == 0 else hst_sb[:, dc * N + n_: dc * N + n_ + 1]
                            v.tensor_tensor_scan(hb, dA_t[(n_ * 8 + dc) % 12][:, :], hb,
                                                 init, OP.mult, OP.add)
                        _v2 = v.tensor_copy(hst3[:, d0:d0 + 4, n_:n_ + 1],
                                            HBg3[:, 0:4, LC - 1:LC])
                        if own:
                            if n_ == 0:
                                _v2 = v.tensor_tensor(y_v[:, d0 * LC:(d0 + 4) * LC], HBg[:, :],
                                                      Crep[n_ % 2][:, :], OP.mult)
                            else:
                                v.tensor_tensor(ytmp[:, :], HBg[:, :], Crep[n_ % 2][:, :], OP.mult)
                                _v2 = v.tensor_add(y_v[:, d0 * LC:(d0 + 4) * LC],
                                                   y_v[:, d0 * LC:(d0 + 4) * LC], ytmp[:, :])
                        sV.inc(_v2)
                    SCAN_V[n_] = sV.v

                emit_bcast(0); emit_rep(0); emit_dA_half(0, 0); emit_dA_half(0, 1)
                emit_bcast(1); emit_rep(1); emit_dA_half(1, 0)
                for n_ in range(N):
                    emit_scan(n_)
                    if n_ + 1 < N:
                        emit_dA_half(n_ + 1, 1)
                    if n_ + 2 < N:
                        emit_bcast(n_ + 2); emit_rep(n_ + 2); emit_dA_half(n_ + 2, 0)
                if lc == NLC // 2 - 1:
                    sV.inc(v.tensor_scalar(hst_sb[:, :], hst_sb[:, :], al_sb[:, 0:1], None, OP.mult))

                if own:
                    loff = (lc - NLC // 2) * LC
                    z_done = []
                    for dc in range(DC):
                        zp = dc % 2
                        s.wait_ge(dX[0].h, XP_RD)
                        s.wait_ge(sV.h, SCAN_V[N - 1])
                        if dc >= 1:
                            s.wait_ge(sV.h, sV.v)
                        sS.inc(s.activation(z_t[zp][:, :], xprd[:, dc * LC:(dc + 1) * LC], AF.Silu))
                        z_done.append(sS.v)
                        v.wait_ge(sS.h, z_done[-1])
                        if dc == 0 and SD_PREV[0] > 0:
                            v.wait_ge(dSd.h, SD_PREV[0])
                        v.scalar_tensor_tensor(y_v[:, dc * LC:(dc + 1) * LC], u_v[:, dc * LC:(dc + 1) * LC],
                                               Dv_sb[:, dc:dc + 1], y_v[:, dc * LC:(dc + 1) * LC],
                                               OP.mult, OP.add)
                        sV.inc(v.tensor_tensor(s_tmp[:, dc * LC:(dc + 1) * LC],
                                               y_v[:, dc * LC:(dc + 1) * LC], z_t[zp][:, :], OP.mult))
                    g.wait_ge(sV.h, sV.v)
                    if dSd.v > 0:
                        g.wait_ge(dSd.h, dSd.v)
                    dma(g, dSd, sd3[:, :, loff:loff + LC], st3d[:, :, :],
                        self_wait=False, accum_op=OP.add)
                    SD_PREV[0] = dSd.v
        g.wait_ge(dSd.h, dSd.v)
    nc.all_engine_barrier()
    eb_sb = sb("eb_sb", [128, E], F32)
    if _dbg:
        if dOp[0].v > 0:
            sy.wait_ge(dOp[0].h, dOp[0].v)
        dma(sy, dOp[0], dbg_hst.ap()[:, :], hst_sb[:, :], self_wait=False)
        sy.wait_ge(dOp[0].h, dOp[0].v)
        if _upto <= 4:
            es0.close(); return nc

    with ExitStack() as ph:
        psb = lambda n_, sh, dt_: ph.enter_context(nc.sbuf_tensor(n_, sh, dt_))
        ct_sb = psb("ct_sb", [128, DC * E], BF16)
        cwf_sb = psb("cwf_sb", [64, T], BF16)
        xnr = [psb(f"xnr{i}", [128, DC * 128], BF16) for i in range(2)]
        aff_sb = psb("aff_sb", [128, E], F32)
        bia_sb = psb("bia_sb", [128, E], F32)
        m8_sb = psb("m8_sb", [128, 8], F32)
        msk_sb = psb("msk_sb", [128, E], F32)
        sg_sb = psb("sg_sb", [128, E], F32)
        wm_sb = psb("wm_sb", [128, E], F32)
        dn_sb = psb("dn_sb", [128, 1], F32)
        dn2_sb = psb("dn2_sb", [128, 1], F32)
        cwt_sb = psb("cwt_sb", [128, E], F32)
        afP = [ph.enter_context(nc.psum_tensor(f"afP{i}", [128, E], F32)) for i in range(2)]
        cwT = [ph.enter_context(nc.psum_tensor(f"cwT{i}", [128, 128], F32)) for i in range(2)]
        if dCg.v > 0:
            g.wait_ge(dCg.h, dCg.v)
        dma(g, dCg, ct_sb[:, :], ctW.ap()[:, :], self_wait=False)
        CT_D = dCg.v
        EB_D = dma(sy, dC, eb_sb[:, :], bass.AP(ebv, 0, [[0, 128], [1, E]]))
        xnd3r = xnd.ap()[:, :].rearrange("(dc p) t -> p dc t", p=128)
        mm_t = []
        for ti in range(16):
            par = ti % 2
            if ti >= 2:
                sy.wait_ge(sT.h, mm_t[ti - 2])
            XLD = dma(sy, dX[par], xnr[par][:, :].rearrange("p (dc t) -> p dc t", dc=DC),
                      xnd3r[:, :, ti * 128:(ti + 1) * 128])
            t.wait_ge(dX[par].h, XLD)
            t.wait_ge(dCg.h, CT_D)
            if ti >= 2:
                t.wait_ge(sV.h, sV.v)
                t.wait_ge(sS.h, sS.v)
            _mi = None
            for dc in range(DC):
                _mi = t.matmul(afP[par][:, :], xnr[par][:, dc * 128:(dc + 1) * 128],
                               ct_sb[:, dc * E:(dc + 1) * E], start=(dc == 0), stop=(dc == DC - 1))
            sT.inc(_mi); mm_t.append(sT.v)
            v.wait_ge(sT.h, sT.v)
            if ti == 0:
                v.wait_ge(dC.h, EB_D)
            v.tensor_copy(aff_sb[:, :], afP[par][:, :])
            v.tensor_add(bia_sb[:, :], aff_sb[:, :], eb_sb[:, :])
            sV.inc(v.max(m8_sb[:, :], bia_sb[:, :]))
            v.wait_ge(sV.h, sV.v)
            sV.inc(v.tensor_scalar(msk_sb[:, :], bia_sb[:, :], m8_sb[:, 7:8], None, OP.is_ge))
            s.wait_ge(sV.h, sV.v)
            sS.inc(s.activation(sg_sb[:, :], aff_sb[:, :], AF.Sigmoid))
            v.wait_ge(sS.h, sS.v)
            v.tensor_mul(wm_sb[:, :], msk_sb[:, :], sg_sb[:, :])
            sV.inc(v.reduce_sum(dn_sb[:, :], wm_sb[:, :], axis=mybir.AxisListType.X))
            v.wait_ge(sV.h, sV.v)
            v.tensor_single_scalar(dn_sb[:, :], dn_sb[:, :], 1e-8, OP.add)
            sV.inc(v.reciprocal(dn2_sb[:, :], dn_sb[:, :]))
            v.wait_ge(sV.h, sV.v)
            sV.inc(v.tensor_single_scalar(cwt_sb[:, :], wm_sb[:, :], dn2_sb[:, 0:1], OP.mult))
            t.wait_ge(sV.h, sV.v)
            sT.inc(t.transpose(cwT[par][0:64, :], cwt_sb[:, 0:64], id_sb[:, :]))
            s.wait_ge(sT.h, sT.v)
            sS.inc(s.copy(cwf_sb[0:64, ti * 128:(ti + 1) * 128], cwT[par][0:64, :]))
        v.wait_ge(sS.h, sS.v)
        sy.wait_ge(sS.h, sS.v)
        dma(sy, dC, cwfd.ap()[:, :], cwf_sb[0:64, :])
        sy.wait_ge(dC.h, dC.v)
        if _dbg:
            cw32 = psb("cw32", [64, T], F32)
            s.wait_ge(sS.h, sS.v)
            sS.inc(s.copy(cw32[0:64, :], cwf_sb[0:64, :]))
            sy.wait_ge(sS.h, sS.v)
            if dOp[0].v > 0:
                sy.wait_ge(dOp[0].h, dOp[0].v)
            dma(sy, dOp[0], dbg_cw.ap()[:, :], cw32[0:64, :], self_wait=False)
            sy.wait_ge(dOp[0].h, dOp[0].v)
    nc.all_engine_barrier()
    if _dbg and _upto <= 5:
        es0.close(); return nc

    C5 = 512
    NC5 = T // C5
    with ExitStack() as ph:
        psb = lambda n_, sh, dt_: ph.enter_context(nc.sbuf_tensor(n_, sh, dt_))
        wu = [psb(f"wu{i}", [128, DC * F], BF16) for i in range(4)]
        wd = [psb(f"wd{i}", [128, 4 * D], BF16) for i in range(4)]
        wshu = psb("wshu", [128, DC * F], BF16)
        wshd = psb("wshd", [128, 4 * D], BF16)
        xnmv = [psb(f"xnmv{i}", [128, DC * C5], BF16) for i in range(2)]
        h_sb = [psb(f"h_sb{i}", [128, 4 * C5], BF16) for i in range(2)]
        hsh_sb = psb("hsh_sb", [128, 4 * C5], BF16)
        cwb_sb = [[psb(f"cwb_sb{i}_{j}", [128, C5], BF16) for j in range(2)] for i in range(2)]
        mt = psb("mt", [128, DC * C5], F32)
        UPr = [ph.enter_context(nc.psum_tensor(f"UPr{i}", [128, C5], F32)) for i in range(2)]
        RTr = [ph.enter_context(nc.psum_tensor(f"RTr{i}", [128, C5], F32)) for i in range(5)]

        if dCg.v > 0:
            g.wait_ge(dCg.h, dCg.v)
        dma(g, dCg, wshu[:, :].rearrange("p (ki f) -> p ki f", ki=DC),
            wsu.ap()[:, :].rearrange("(ki p) f -> p ki f", p=128), self_wait=False)
        dma(g, dCg, wshd[:, :].rearrange("p (fk d) -> p fk d", fk=4),
            wsd.ap()[:, :].rearrange("(fk p) d -> p fk d", p=128), self_wait=False)
        SH_D = dCg.v

        def load_expert(e):
            sl = e % 4
            if dW[sl].v > 0:
                g.wait_ge(dW[sl].h, dW[sl].v)
            dma(g, dW[sl], wu[sl][:, :].rearrange("p (ki f) -> p ki f", ki=DC),
                wup.ap()[e, :, :].rearrange("(ki p) f -> p ki f", p=128), self_wait=False)
            dma(g, dW[sl], wd[sl][:, :].rearrange("p (fk d) -> p fk d", fk=4),
                wdn.ap()[e, :, :].rearrange("(fk p) d -> p fk d", p=128), self_wait=False)
            return dW[sl].v

        wready = {0: load_expert(0), 1: load_expert(1)}
        xnd3r = xnd.ap()[:, :].rearrange("(dc p) t -> p dc t", p=128)
        rtd3 = rtd.ap()[:, :].rearrange("(dc p) t -> p dc t", p=128)

        n_it = 32 * NC5
        tile_T = {}

        def xn_load(it_):
            c5_ = it_ % NC5
            if it_ >= 2:
                sy.wait_ge(sT.h, tile_T[it_ - 2])
            return dma(sy, dX[it_ % 2], xnmv[it_ % 2][:, :].rearrange("p (dc t) -> p dc t", dc=DC),
                       xnd3r[:, :, c5_ * C5:(c5_ + 1) * C5])

        def cwb_load(it_):
            grp_, c5_ = it_ // NC5, it_ % NC5
            ee0, ee1 = 2 * grp_, 2 * grp_ + 1
            if dSp[0].v > 0:
                sy.wait_ge(sV.h, hs_done.get(it_ - 2, 0))
                sy.wait_ge(dSp[0].h, dSp[0].v)
                sy.wait_ge(dSp[1].h, dSp[1].v)
            dma(sy, dSp[0], cwb_sb[0][it_ % 2][:, :],
                bass.AP(cwfd, ee0 * T + c5_ * C5, [[0, 128], [1, C5]]), self_wait=False)
            dma(sy, dSp[1], cwb_sb[1][it_ % 2][:, :],
                bass.AP(cwfd, ee1 * T + c5_ * C5, [[0, 128], [1, C5]]), self_wait=False)
            return (dSp[0].v, dSp[1].v)

        hs_done = {}
        xn_rd = {0: xn_load(0), 1: xn_load(1)}
        cw_rd = {0: cwb_load(0)}
        expert_down_T = {}
        ev_ring = {}
        dma_it = {}
        it = 0
        for grp in range(32):
            e0, e1 = 2 * grp, 2 * grp + 1
            for c5 in range(NC5):
                par = it % 2

                def up_expert(wsrc, hdst, silu_waits_extra=0):
                    sil = []
                    for fc in range(4):
                        up = UPr[fc % 2]
                        if fc >= 2:
                            t.wait_ge(sS.h, sil[fc - 2])
                        _mi = None
                        for ki in range(DC):
                            _mi = t.matmul(up[:, :], wsrc[:, ki * F + fc * 128: ki * F + (fc + 1) * 128],
                                           xnmv[par][:, ki * C5:(ki + 1) * C5],
                                           start=(ki == 0), stop=(ki == DC - 1))
                        sT.inc(_mi)
                        s.wait_ge(sT.h, sT.v)
                        sS.inc(s.activation(hdst[:, fc * C5:(fc + 1) * C5], up[:, :], AF.Silu))
                        sil.append(sS.v)
                    return sil[-1]

                t.wait_ge(dX[par].h, xn_rd[it])
                if it + 1 < n_it and (it + 1) not in cw_rd:
                    cw_rd[it + 1] = cwb_load(it + 1)
                if grp == 0:
                    t.wait_ge(dCg.h, SH_D)
                    SHS = up_expert(wshu, hsh_sb)
                t.wait_ge(dW[e0 % 4].h, wready[e0])
                S0 = up_expert(wu[e0 % 4], h_sb[0])
                v.wait_ge(dSp[0].h, cw_rd[it][0])
                v.wait_ge(sS.h, S0)
                _vi = None
                for fc in range(4):
                    _vi = v.tensor_tensor(h_sb[0][:, fc * C5:(fc + 1) * C5], h_sb[0][:, fc * C5:(fc + 1) * C5],
                                          cwb_sb[0][it % 2][:, :], OP.mult)
                sV.inc(_vi); HS0 = sV.v
                t.wait_ge(dW[e1 % 4].h, wready[e1])
                S1 = up_expert(wu[e1 % 4], h_sb[1])
                v.wait_ge(dSp[1].h, cw_rd[it][1])
                v.wait_ge(sS.h, S1)
                _vi = None
                for fc in range(4):
                    _vi = v.tensor_tensor(h_sb[1][:, fc * C5:(fc + 1) * C5], h_sb[1][:, fc * C5:(fc + 1) * C5],
                                          cwb_sb[1][it % 2][:, :], OP.mult)
                sV.inc(_vi); HS1 = sV.v
                hs_done[it] = sV.v
                t.wait_ge(sV.h, HS0)
                hs1_waited = False
                halfA_ev = None
                for j, dc in enumerate(range(DC)):
                    slot = j % 5
                    if (it, slot) != (0, slot) and slot in ev_ring:
                        eng_sem, cnt = ev_ring[slot]
                        t.wait_ge(eng_sem, cnt)
                    first_srcs = ([(wshd, hsh_sb)] if grp == 0 else []) + [(wd[e0 % 4], h_sb[0]), (wd[e1 % 4], h_sb[1])]
                    _mi = None
                    nsrc = len(first_srcs)
                    for si, (wsrc, hsrc) in enumerate(first_srcs):
                        if hsrc is h_sb[1] and not hs1_waited:
                            t.wait_ge(sV.h, HS1)
                            hs1_waited = True
                        for fk in range(4):
                            _mi = t.matmul(RTr[slot][:, :],
                                           wsrc[:, fk * D + dc * 128: fk * D + (dc + 1) * 128],
                                           hsrc[:, fk * C5:(fk + 1) * C5],
                                           start=(si == 0 and fk == 0), stop=(si == nsrc - 1 and fk == 3))
                    sT.inc(_mi); DN = sT.v
                    if dc % 2 == 0:
                        v.wait_ge(sT.h, DN)
                        if it >= 1:
                            v.wait_ge(dR[it % 2].h, dR[it % 2].v)
                        sV.inc(v.tensor_copy(mt[:, dc * C5:(dc + 1) * C5], RTr[slot][:, :]))
                        ev_ring[slot] = (sV.h, sV.v)
                    else:
                        s.wait_ge(sT.h, DN)
                        if it >= 1:
                            s.wait_ge(dR[it % 2].h, dR[it % 2].v)
                        sS.inc(s.copy(mt[:, dc * C5:(dc + 1) * C5], RTr[slot][:, :]))
                        ev_ring[slot] = (sS.h, sS.v)
                    if dc == 3:
                        halfA_ev = (sV.v, sS.v)
                    expert_down_T[e0] = DN
                    expert_down_T[e1] = DN
                tile_T[it] = sT.v
                if it + 2 < n_it:
                    xn_rd[it + 2] = xn_load(it + 2)
                aop = OP.bypass if grp == 0 else OP.add
                g.wait_ge(sV.h, halfA_ev[0])
                g.wait_ge(sS.h, halfA_ev[1])
                if dR[it % 2].v > 0:
                    g.wait_ge(dR[it % 2].h, dR[it % 2].v)
                dma(g, dR[it % 2], rtd3[:, 0:4, c5 * C5:(c5 + 1) * C5],
                    mt[:, 0:4 * C5].rearrange("p (dc t) -> p dc t", dc=4), self_wait=False, accum_op=aop)
                g.wait_ge(sV.h, sV.v)
                g.wait_ge(sS.h, sS.v)
                dma(g, dR[it % 2], rtd3[:, 4:8, c5 * C5:(c5 + 1) * C5],
                    mt[:, 4 * C5:8 * C5].rearrange("p (dc t) -> p dc t", dc=4), self_wait=False, accum_op=aop)
                dma_it[it] = dR[it % 2].v
                it += 1
                if c5 == 0:
                    for en in (2 * grp + 2, 2 * grp + 3):
                        if en < 64 and en not in wready:
                            g.wait_ge(sT.h, expert_down_T.get(en - 4, 0))
                            wready[en] = load_expert(en)
        sy.wait_ge(dR[0].h, dR[0].v)
        sy.wait_ge(dR[1].h, dR[1].v)
    nc.all_engine_barrier()
    if _dbg and _upto <= 6:
        es0.close(); return nc

    with ExitStack() as ph:
        psb = lambda n_, sh, dt_: ph.enter_context(nc.sbuf_tensor(n_, sh, dt_))
        rrd = [psb(f"rrd{i}", [128, DC * TC], F32) for i in range(2)]
        srd = [psb(f"srd{i}", [128, DC * TC], F32) for i in range(2)]
        g_sb = psb("g2_sb", [128, DC * TC], F32)
        sg_sb = psb("sg2_sb", [128, DC * TC], F32)
        xrd = [psb(f"xrd{i}", [128, D], F32) for i in range(2)]
        ob = [psb(f"ob{i}", [128, D], F32) for i in range(2)]
        EP = [ph.enter_context(nc.psum_tensor(f"EP{i}", [128, D], F32)) for i in range(2)]
        rtd3 = rtd.ap()[:, :].rearrange("(dc p) t -> p dc t", p=128)
        sg3 = sg_sb[:, :].rearrange("p (dc t) -> p dc t", dc=DC)
        sil_s, mul_v = [], []
        for tc in range(NTC):
            par = tc % 2
            if tc >= 2:
                g.wait_ge(sS.h, sil_s[tc - 2])
                g.wait_ge(sV.h, mul_v[tc - 2])
            RLD = dma(g, dGx[par], rrd[par][:, :].rearrange("p (dc t) -> p dc t", dc=DC),
                      rtd3[:, :, tc * TC:(tc + 1) * TC])
            SLD = dma(g, dSg[par], srd[par][:, :].rearrange("p (dc t) -> p dc t", dc=DC),
                      sd3[:, :, tc * TC:(tc + 1) * TC])
            s.wait_ge(dGx[par].h, RLD)
            if tc >= 1:
                s.wait_ge(sV.h, sV.v)
            sS.inc(s.activation(g_sb[:, :], rrd[par][:, :], AF.Silu)); sil_s.append(sS.v)
            v.wait_ge(sS.h, sS.v)
            v.wait_ge(dSg[par].h, SLD)
            v.wait_ge(sT.h, sT.v)
            sV.inc(v.tensor_mul(sg_sb[:, :], srd[par][:, :], g_sb[:, :]))
            mul_v.append(sV.v)
            for tsub in range(2):
                tpar = tsub
                trow = tc * TC + tsub * 128
                XLD = dma(sy, dSp[tpar], xrd[tpar][:, :], x_own.ap()[trow:trow + 128, :])
                t.wait_ge(sV.h, sV.v)
                _mi = None
                for dc in range(DC):
                    _mi = t.transpose(EP[tpar][:, dc * 128:(dc + 1) * 128],
                                      sg3[:, dc, tsub * 128:(tsub + 1) * 128], id_sb[:, :])
                sT.inc(_mi)
                v.wait_ge(sT.h, sT.v)
                v.wait_ge(dSp[tpar].h, XLD)
                if dOp[tpar].v > 0:
                    v.wait_ge(dOp[tpar].h, dOp[tpar].v)
                sV.inc(v.tensor_add(ob[tpar][:, :], xrd[tpar][:, :], EP[tpar][:, :]))
                sy.wait_ge(sV.h, sV.v)
                dma(sy, dOp[tpar], out.ap()[trow:trow + 128, :], ob[tpar][:, :])
        sy.wait_ge(dOp[0].h, dOp[0].v)
        sy.wait_ge(dOp[1].h, dOp[1].v)
    es0.close()
    return nc


def _pk_vec(a):
    return np.ascontiguousarray(np.asarray(a, np.float32).reshape(8, 128).T)


def _pk_km(a_km, kchunks=8):
    K_, M_ = a_km.shape
    return np.ascontiguousarray(a_km.reshape(kchunks, 128, M_).transpose(1, 0, 2).reshape(128, kchunks * M_))


def _ext_dbw(dbw):
    dbwt = dbw.T
    ext = np.concatenate([dbwt[:, 64:96], dbwt[:, 64:96], dbwt[:, 0:64]], axis=1)
    return _pk_km(np.ascontiguousarray(ext))


_prog = None


def kernel(**inputs):
    global _prog
    if _prog is None:
        _prog = build_nc()
    nc = _prog

    f32 = lambda a: np.ascontiguousarray(np.asarray(a, dtype=np.float32))
    x = f32(inputs["L0_attn_x"])
    common = dict(
        pjW=f32(inputs["L0_attn_proj_W"]).T.copy(),
        fcW=f32(inputs["L0_attn_fconv_W"]).T.copy(),
        bcW=f32(inputs["L0_attn_bconv_W"]).T.copy(),
        dbW=_ext_dbw(f32(inputs["L0_attn_deltaBC_W"])),
        dtW=f32(inputs["L0_attn_dt_W"]).T.copy(),
        Alog=_pk_km(f32(inputs["L0_attn_A_log"])),
        Dv=_pk_vec(inputs["L0_attn_D"]),
        lnw=_pk_vec(inputs["L0_attn_ln_w"]), lnb=_pk_vec(inputs["L0_attn_ln_b"]),
        pjb=_pk_vec(inputs["L0_attn_proj_b"]), fcb=_pk_vec(inputs["L0_attn_fconv_b"]),
        bcb=_pk_vec(inputs["L0_attn_bconv_b"]), dtb=_pk_vec(inputs["L0_attn_dt_b"]),
        ctW=_pk_km(f32(inputs["L0_ffn_centroids"]).T.copy()),
        ebv=f32(inputs["L0_ffn_expert_biases"]),
        wsu=f32(inputs["L0_ffn_Ws_up"]), wsd=f32(inputs["L0_ffn_Ws_down"]),
        wup=f32(inputs["L0_ffn_W_up"]), wdn=f32(inputs["L0_ffn_W_down"]),
    )
    in_maps = []
    for c in range(8):
        b, h = c // 2, c % 2
        xo = np.ascontiguousarray(x[b, h * T:(h + 1) * T, :])
        xp = np.ascontiguousarray(x[b, 0:T, :]) if h == 1 else xo
        al = np.full((128, 1), float(h), np.float32)
        in_maps.append(dict(common, x_own=xo, x_pre=xp, alpha=al))

    res = run_bass_kernel_spmd(nc, in_maps, core_ids=list(range(8)))
    outp = np.empty((B, S, D), dtype=np.float32)
    for c in range(8):
        b, h = c // 2, c % 2
        outp[b, h * T:(h + 1) * T, :] = res.results[c]["out"]
    return outp


# version 13
# speedup vs baseline: 1.0101x; 1.0101x over previous
import numpy as np
import ml_dtypes
from contextlib import ExitStack

import concourse.bass as bass
import concourse.mybir as mybir
from concourse.bass_utils import run_bass_kernel_spmd

F32 = mybir.dt.float32
BF16 = mybir.dt.bfloat16
OP = mybir.AluOpType
AF = mybir.ActivationFunctionType
BF = ml_dtypes.bfloat16

B, S, D = 4, 4096, 1024
T = 2048
N = 16
E, F = 64, 512
DC = 8
LC = 512
NLC = (2 * T) // LC
TC = 256
NTC = T // TC
EPS = 1e-5


class Sem:
    def __init__(self, nc, es, name):
        self.h = es.enter_context(nc.semaphore(name))
        self.v = 0

    def inc(self, instr, k=1):
        instr.then_inc(self.h, k)
        self.v += k
        return instr


def build_nc(_upto=99, _dbg=False):
    nc = bass.Bass("TRN2", target_bir_lowering=False, detect_race_conditions=False)
    g, v, t, s, sy = nc.gpsimd, nc.vector, nc.tensor, nc.scalar, nc.sync

    dI = lambda n_, sh: nc.dram_tensor(n_, sh, F32, kind="ExternalInput")
    x_own = dI("x_own", [T, D])
    x_pre = dI("x_pre", [T, D])
    alpha = dI("alpha", [128, 1])
    pjW = dI("pjW", [D, D])
    fcW = dI("fcW", [D, D])
    bcW = dI("bcW", [D, D])
    dbW = dI("dbW", [128, DC * 128])
    dtW = dI("dtW", [64, D])
    Alog = dI("Alog", [128, DC * N])
    Dv = dI("Dv", [128, DC])
    lnw = dI("lnw", [128, DC]); lnb = dI("lnb", [128, DC])
    pjb = dI("pjb", [128, DC]); fcb = dI("fcb", [128, DC]); bcb = dI("bcb", [128, DC]); dtb = dI("dtb", [128, DC])
    ctW = dI("ctW", [128, DC * E])
    ebv = dI("ebv", [E])
    wsu = dI("wsu", [D, F])
    wsd = dI("wsd", [F, D])
    wup = dI("wup", [E, D, F])
    wdn = dI("wdn", [E, F, D])
    out = nc.dram_tensor("out", [T, D], F32, kind="ExternalOutput")

    _ik = "ExternalOutput" if _dbg else "Internal"
    xn2d = nc.dram_tensor("xn2d", [D, 2 * T], BF16, kind=_ik)
    xnd = nc.dram_tensor("xnd", [D, T], BF16, kind=_ik)
    xpd = nc.dram_tensor("xpd", [D, 2 * T], BF16, kind=_ik)
    rtd = nc.dram_tensor("rtd", [D, T], F32, kind=_ik)
    sd = nc.dram_tensor("sd", [128, DC * T], F32, kind=_ik)
    cwfd = nc.dram_tensor("cwfd", [64, T], BF16, kind="Internal")
    if _dbg:
        dbg_sacc = nc.dram_tensor("dbg_sacc", [128, DC * T], F32, kind="ExternalOutput")
        dbg_cw = nc.dram_tensor("dbg_cw", [64, T], F32, kind="ExternalOutput")
        dbg_hst = nc.dram_tensor("dbg_hst", [128, 128], F32, kind="ExternalOutput")
        dbg_st = nc.dram_tensor("dbg_st", [128, 8 * 32], F32, kind="ExternalOutput")
        dbg_st2 = nc.dram_tensor("dbg_st2", [128, 8 * 32], F32, kind="ExternalOutput")
        dbg_lnw = nc.dram_tensor("dbg_lnw", [128, DC], F32, kind="ExternalOutput")

    identI = nc.inline_tensor(np.eye(128, dtype=np.float32), "identI")
    ohbc_np = np.zeros((128, 16 * 128), dtype=np.float32)
    for s_ in range(32):
        base, colb = (s_ % 2) * 32, (s_ // 2) * 128
        ohbc_np[base + s_, colb:colb + 128] = 1.0
    ohbcI = nc.inline_tensor(ohbc_np, "ohbcI")
    ohe_np = np.zeros((64, 64 * 128), dtype=BF)
    for e_ in range(64):
        ohe_np[e_, e_ * 128:(e_ + 1) * 128] = 1.0
    oheI = nc.inline_tensor(ohe_np, "oheI")

    es0 = ExitStack()
    sb = lambda n_, sh, dt_: es0.enter_context(nc.sbuf_tensor(n_, sh, dt_))

    A_sb = sb("A_sb", [128, DC * N], F32)
    Dv_sb = sb("Dv_sb", [128, DC], F32)
    lnw_sb = sb("lnw_sb", [128, DC], F32)
    lnb_sb = sb("lnb_sb", [128, DC], F32)
    pjb_sb = sb("pjb_sb", [128, DC], F32)
    fcb_sb = sb("fcb_sb", [128, DC], F32)
    bcb_sb = sb("bcb_sb", [128, DC], F32)
    dtb_sb = sb("dtb_sb", [128, DC], F32)
    al_sb = sb("al_sb", [128, 1], F32)
    id_sb = sb("id_sb", [128, 128], F32)
    hst_sb = sb("hst_sb", [128, 128], F32)
    dbc_sb = sb("dbc_sb", [128, LC], F32)
    dbWsb = sb("dbWsb", [128, DC * 128], F32)
    dtWsb = sb("dtWsb", [128, D], F32)
    st_sb = sb("st_sb", [128, 8 * 32], F32)
    eps_sb = sb("eps_sb", [128, 1], F32)

    sT = Sem(nc, es0, "sT"); sV = Sem(nc, es0, "sV"); sS = Sem(nc, es0, "sS")
    dC = Sem(nc, es0, "dC")
    dCg = Sem(nc, es0, "dCg")
    dX = [Sem(nc, es0, f"dX{i}") for i in range(2)]
    dSp = [Sem(nc, es0, f"dSp{i}") for i in range(2)]
    dW = [Sem(nc, es0, f"dW{i}") for i in range(4)]
    dR = [Sem(nc, es0, f"dR{i}") for i in range(2)]
    dOp = [Sem(nc, es0, f"dOp{i}") for i in range(2)]
    dSd = Sem(nc, es0, "dSd")
    dGx = [Sem(nc, es0, f"dGx{i}") for i in range(2)]
    dSg = [Sem(nc, es0, f"dSg{i}") for i in range(2)]

    def dma(eng, sem, out_, in_, self_wait=True, **kw):
        if self_wait and sem.v > 0:
            eng.wait_ge(sem.h, sem.v)
        eng.dma_start(out=out_, in_=in_, **kw).then_inc(sem.h, 16)
        sem.v += 16
        return sem.v

    dma(sy, dC, A_sb[:, :], Alog.ap()[:, :], self_wait=False)
    for dram, tile in [(Dv, Dv_sb), (lnw, lnw_sb), (lnb, lnb_sb), (pjb, pjb_sb),
                       (fcb, fcb_sb), (bcb, bcb_sb), (dtb, dtb_sb)]:
        dma(sy, dC, tile[:, :], dram.ap()[:, :], self_wait=False)
    dma(sy, dC, al_sb[:, :], alpha.ap()[:, :], self_wait=False)
    dma(sy, dC, dbWsb[:, :], dbW.ap()[:, :], self_wait=False)
    dma(sy, dC, dtWsb[64:128, :], dtW.ap()[:, :], self_wait=False)
    dma(sy, dC, id_sb[:, :], identI.ap()[:, :], self_wait=False)
    P0_LOADS = dC.v
    s.wait_ge(dC.h, P0_LOADS)
    sS.inc(s.activation(A_sb[:, :], A_sb[:, :], AF.Exp))
    v.wait_ge(sS.h, sS.v)
    v.tensor_single_scalar(A_sb[:, :], A_sb[:, :], -1.0, OP.mult)
    v.memset(dbc_sb[:, :], 0.0)
    v.memset(eps_sb[:, :], EPS)
    sV.inc(v.engine_nop())
    V_INIT = sV.v

    stv = st_sb[:, :].rearrange("p (k c) -> p k c", k=8)
    ST_S, ST_Q, ST_MU, ST_V, ST_R, ST_RR2, ST_T1, ST_T2 = range(8)
    stc = lambda k, ti: st_sb[:, k * 32 + ti: k * 32 + ti + 1]

    xsrc = lambda ti: (x_pre if ti < 16 else x_own).ap()[(ti % 16) * 128:(ti % 16) * 128 + 128, :]
    sd3 = sd.ap()[:, :].rearrange("p (dc t) -> p dc t", dc=DC)
    with ExitStack() as ph:
        psb = lambda n_, sh, dt_: ph.enter_context(nc.sbuf_tensor(n_, sh, dt_))
        pps = lambda n_, sh: ph.enter_context(nc.psum_tensor(n_, sh, F32))
        xbuf = [psb(f"xbuf{i}", [128, D], F32) for i in range(2)]
        sq_scr = psb("sq_scr", [128, D], F32)
        xn_tm = [psb(f"xn_tm{i}", [128, D], F32) for i in range(2)]
        xr_tm = [psb(f"xr_tm{i}", [128, D], F32) for i in range(2)]
        xn2b = [psb(f"xn2b{i}", [128, D], BF16) for i in range(2)]
        xnbb = [psb(f"xnbb{i}", [128, D], BF16) for i in range(2)]
        sstg = [psb(f"sstg{i}", [128, D], F32) for i in range(2)]
        trA = [pps(f"trA{i}", [128, D]) for i in range(2)]
        trB = [pps(f"trB{i}", [128, D]) for i in range(2)]

        v_tile, s_tile = [], []
        for ti in range(32):
            par = ti % 2
            if ti >= 2:
                g.wait_ge(sV.h, v_tile[ti - 2])
                g.wait_ge(sS.h, s_tile[ti - 2])
            XLD = dma(g, dGx[par], xbuf[par][:, :], xsrc(ti))
            v.wait_ge(dGx[par].h, XLD)
            if ti == 0:
                v.wait_ge(sV.h, V_INIT)
            sV.inc(v.reduce_sum(stc(ST_S, ti), xbuf[par][:, :], axis=mybir.AxisListType.X))
            v_tile.append(sV.v)
            s.wait_ge(dGx[par].h, XLD)
            sS.inc(s.activation(sq_scr[:, :], xbuf[par][:, :], AF.Square, accum_out=stc(ST_Q, ti)))
            s_tile.append(sS.v)
        v.wait_ge(sS.h, sS.v)
        v.tensor_single_scalar(stv[:, ST_MU, :], stv[:, ST_S, :], 1.0 / D, OP.mult)
        v.tensor_single_scalar(stv[:, ST_T1, :], stv[:, ST_Q, :], 1.0 / D, OP.mult)
        v.tensor_mul(stv[:, ST_T2, :], stv[:, ST_MU, :], stv[:, ST_MU, :])
        sV.inc(v.tensor_sub(stv[:, ST_V, :], stv[:, ST_T1, :], stv[:, ST_T2, :]))
        s.wait_ge(sV.h, sV.v)
        sS.inc(s.activation(stv[:, ST_T1, :], stv[:, ST_V, :], AF.Sqrt, bias=eps_sb[:, 0:1]))
        v.wait_ge(sS.h, sS.v)
        sV.inc(v.reciprocal(stv[:, ST_R, :], stv[:, ST_T1, :]))
        v.wait_ge(sV.h, sV.v)
        v.tensor_mul(stv[:, ST_T1, :], stv[:, ST_V, :], stv[:, ST_R, :])
        sV.inc(v.tensor_mul(stv[:, ST_T2, :], stv[:, ST_T1, :], stv[:, ST_R, :]))
        s.wait_ge(sV.h, sV.v)
        sS.inc(s.activation(stv[:, ST_T1, :], stv[:, ST_T2, :], AF.Sqrt, bias=eps_sb[:, 0:1]))
        v.wait_ge(sS.h, sS.v)
        sV.inc(v.reciprocal(stv[:, ST_T2, :], stv[:, ST_T1, :]))
        v.wait_ge(sV.h, sV.v)
        sV.inc(v.tensor_mul(stv[:, ST_RR2, :], stv[:, ST_R, :], stv[:, ST_T2, :]))
        if _dbg:
            sy.wait_ge(sV.h, sV.v)
            dma(sy, dOp[1], dbg_st2.ap()[:, :], st_sb[:, :], self_wait=False)
            sy.wait_ge(dOp[1].h, dOp[1].v)

        xn2d3 = xn2d.ap()[:, :].rearrange("(dc p) t -> p dc t", p=128)
        xnd3 = xnd.ap()[:, :].rearrange("(dc p) t -> p dc t", p=128)
        ev_v, ev_s, sp_done, sg_done, trd = [], [], [], [], []
        for ti in range(32):
            own = ti >= 16
            par = ti % 2
            if ti >= 2:
                sy.wait_ge(sV.h, ev_v[ti - 2])
            else:
                sy.wait_ge(sV.h, v_tile[-1])
                sy.wait_ge(sS.h, s_tile[-1])
            XLD = dma(sy, dX[par], xbuf[par][:, :], xsrc(ti))
            v.wait_ge(dX[par].h, XLD)
            if ti >= 2:
                v.wait_ge(sT.h, trd[ti - 2])
            v.tensor_scalar(xn_tm[par][:, :], xbuf[par][:, :], stc(ST_MU, ti), stc(ST_R, ti), OP.subtract, OP.mult)
            sV.inc(v.tensor_single_scalar(xr_tm[par][:, :], xn_tm[par][:, :], stc(ST_T2, ti), OP.mult))
            NORM_V = sV.v
            t.wait_ge(sV.h, NORM_V)
            if ti >= 2:
                t.wait_ge(sS.h, ev_s[ti - 2])
                t.wait_ge(sV.h, ev_v[ti - 2])
            _ti = None
            for dc in range(DC):
                _ti = t.transpose(trA[par][:, dc * 128:(dc + 1) * 128], xr_tm[par][:, dc * 128:(dc + 1) * 128], id_sb[:, :])
            if own:
                for dc in range(DC):
                    _ti = t.transpose(trB[par][:, dc * 128:(dc + 1) * 128], xn_tm[par][:, dc * 128:(dc + 1) * 128], id_sb[:, :])
            sT.inc(_ti); trd.append(sT.v)
            s.wait_ge(sT.h, sT.v)
            if ti >= 2:
                s.wait_ge(dSp[par].h, sp_done[ti - 2])
                s.wait_ge(dSg[par].h, sg_done[ti - 2])
            _si = None
            for dc in range(DC):
                _si = s.activation(xn2b[par][:, dc * 128:(dc + 1) * 128], trA[par][:, dc * 128:(dc + 1) * 128],
                                   AF.Identity, bias=lnb_sb[:, dc:dc + 1], scale=lnw_sb[:, dc:dc + 1])
            if own:
                _si = s.copy(xnbb[par][:, :], trB[par][:, :])
            sS.inc(_si); ev_s.append(sS.v)
            v.wait_ge(sT.h, sT.v)
            if own:
                if len(sg_done) >= 2:
                    v.wait_ge(dSg[par].h, sg_done[-2])
                sV.inc(v.tensor_copy(sstg[par][:, :], trB[par][:, :]))
            else:
                sV.inc(v.engine_nop())
            ev_v.append(sV.v)
            sy.wait_ge(sS.h, ev_s[-1])
            if dSp[par].v > 0:
                sy.wait_ge(dSp[par].h, dSp[par].v)
            dma(sy, dSp[par], xn2d3[:, :, ti * 128:(ti + 1) * 128],
                xn2b[par][:, :].rearrange("p (dc t) -> p dc t", dc=DC), self_wait=False)
            sp_done.append(dSp[par].v)
            if own:
                ti2 = ti - 16
                g.wait_ge(sS.h, ev_s[-1])
                g.wait_ge(sV.h, ev_v[-1])
                if dSg[par].v > 0:
                    g.wait_ge(dSg[par].h, dSg[par].v)
                dma(g, dSg[par], xnd3[:, :, ti2 * 128:(ti2 + 1) * 128],
                    xnbb[par][:, :].rearrange("p (dc t) -> p dc t", dc=DC), self_wait=False)
                dma(g, dSg[par], sd3[:, :, ti2 * 128:(ti2 + 1) * 128],
                    sstg[par][:, :].rearrange("p (dc t) -> p dc t", dc=DC), self_wait=False)
            sg_done.append(dSg[par].v)
        sy.wait_ge(dSp[0].h, dSp[0].v)
        sy.wait_ge(dSp[1].h, dSp[1].v)
        sy.wait_ge(dSg[0].h, dSg[0].v)
        sy.wait_ge(dSg[1].h, dSg[1].v)
        if _dbg:
            dma(sy, dOp[0], dbg_st.ap()[:, :], st_sb[:, :], self_wait=False)
            dma(sy, dOp[0], dbg_lnw.ap()[:, :], lnw_sb[:, :], self_wait=False)
            sy.wait_ge(dOp[0].h, dOp[0].v)
    nc.all_engine_barrier()
    if _dbg and _upto <= 1:
        es0.close(); return nc

    xpd3 = xpd.ap()[:, :].rearrange("(dc p) t -> p dc t", p=128)
    xn2d3r = xn2d.ap()[:, :].rearrange("(ki p) t -> p ki t", p=128)
    with ExitStack() as ph:
        psb = lambda n_, sh, dt_: ph.enter_context(nc.sbuf_tensor(n_, sh, dt_))
        wmat = psb("wmat", [128, DC * D], BF16)
        x2rd = [psb(f"x2rd{i}", [128, DC * LC], BF16) for i in range(2)]
        xpb = [psb(f"xpb{i}", [128, LC], BF16) for i in range(2)]
        pj = [ph.enter_context(nc.psum_tensor(f"pj{i}", [128, LC], F32)) for i in range(2)]
        if dCg.v > 0:
            g.wait_ge(dCg.h, dCg.v)
        for ki in range(DC):
            dma(g, dCg, wmat[:, ki * D:(ki + 1) * D], pjW.ap()[ki * 128:(ki + 1) * 128, :], self_wait=False)
        W_DONE = dCg.v
        ev, st_cnt, lc_mm = [], [], []
        for lc in range(NLC):
            par = lc % 2
            if lc >= 2:
                g.wait_ge(sT.h, lc_mm[lc - 2])
            XLD = dma(g, dGx[par], x2rd[par][:, :].rearrange("p (ki t) -> p ki t", ki=DC),
                      xn2d3r[:, :, lc * LC:(lc + 1) * LC])
            t.wait_ge(dGx[par].h, XLD)
            t.wait_ge(dCg.h, W_DONE)
            for mi in range(DC):
                mp = mi % 2
                if len(ev) >= 2:
                    t.wait_ge(sS.h, ev[-2])
                _mi = None
                for ki in range(DC):
                    _mi = t.matmul(pj[mp][:, :], wmat[:, ki * D + mi * 128: ki * D + (mi + 1) * 128],
                                   x2rd[par][:, ki * LC:(ki + 1) * LC], start=(ki == 0), stop=(ki == DC - 1))
                sT.inc(_mi)
                s.wait_ge(sT.h, sT.v)
                if len(st_cnt) >= 2:
                    s.wait_ge(dSp[mp].h, st_cnt[-2])
                sS.inc(s.activation(xpb[mp][:, :], pj[mp][:, :], AF.Identity, bias=pjb_sb[:, mi:mi + 1]))
                ev.append(sS.v)
                sy.wait_ge(sS.h, sS.v)
                dma(sy, dSp[mp], xpd3[:, mi, lc * LC:(lc + 1) * LC], xpb[mp][:, :])
                st_cnt.append(dSp[mp].v)
            lc_mm.append(sT.v)
        sy.wait_ge(dSp[0].h, dSp[0].v)
        sy.wait_ge(dSp[1].h, dSp[1].v)
    nc.all_engine_barrier()
    if _dbg and _upto <= 2:
        es0.close(); return nc

    with ExitStack() as ph:
        psb = lambda n_, sh, dt_: ph.enter_context(nc.sbuf_tensor(n_, sh, dt_))
        wmat = psb("wmatB", [128, DC * D], BF16)
        xprd = psb("xprd", [128, DC * LC], BF16)
        u_v = psb("u_v", [128, DC * LC], F32)
        d_v = psb("d_v", [128, DC * LC], F32)
        w_v = psb("w_v", [128, DC * LC], F32)
        y_v = psb("y_v", [128, DC * LC], F32)
        ohbc_sb = psb("ohbc_sb", [128, 16 * 128], F32)
        sp_scr = psb("sp_scr", [128, LC], F32)
        z_t = [sp_scr, sp_scr]
        dA_t = [psb(f"dA{i}", [128, LC], F32) for i in range(14)]
        HBg = psb("HBg", [128, 4 * LC], F32)
        Brep = [psb(f"Brep{i}", [128, 4 * LC], F32) for i in range(2)]
        Crep = [psb(f"Crep{i}", [128, 4 * LC], F32) for i in range(2)]
        ytmp = psb("ytmp", [128, 4 * LC], F32)
        s_tmp = psb("s_tmp", [128, DC * LC], F32)
        uP = [ph.enter_context(nc.psum_tensor(f"uP{i}", [128, LC], F32)) for i in range(2)]
        dbP = ph.enter_context(nc.psum_tensor("dbP", [128, LC], F32))
        bcP = [ph.enter_context(nc.psum_tensor(f"bcP{i}", [128, 2 * LC], F32)) for i in range(2)]

        if dCg.v > 0:
            g.wait_ge(dCg.h, dCg.v)
        dma(g, dCg, ohbc_sb[:, :], ohbcI.ap()[:, :], self_wait=False)
        xpd3r = xpd.ap()[:, :].rearrange("(dc p) t -> p dc t", p=128)
        st3d = s_tmp[:, :].rearrange("p (dc t) -> p dc t", dc=DC)
        SD_PREV = [0]

        for br, (Wd, bvec) in enumerate([(fcW, fcb_sb), (bcW, bcb_sb)]):
            g.wait_ge(sT.h, sT.v)
            g.wait_ge(dCg.h, dCg.v)
            for ki in range(DC):
                dma(g, dCg, wmat[:, ki * D:(ki + 1) * D], Wd.ap()[ki * 128:(ki + 1) * 128, :], self_wait=False)
            WBR = dCg.v
            for lc in range(NLC):
                own = lc >= NLC // 2
                sy.wait_ge(sV.h, sV.v)
                sy.wait_ge(sS.h, sS.v)
                XP_RD = dma(sy, dX[0], xprd[:, :].rearrange("p (ki t) -> p ki t", ki=DC),
                            xpd3r[:, :, lc * LC:(lc + 1) * LC])
                t.wait_ge(dX[0].h, XP_RD)
                t.wait_ge(dCg.h, WBR)
                s.wait_ge(sV.h, sV.v)
                u_evac = []
                for mi in range(DC):
                    mp = mi % 2
                    _mi = None
                    for ki in range(DC):
                        _mi = t.matmul(uP[mp][:, :], wmat[:, ki * D + mi * 128: ki * D + (mi + 1) * 128],
                                       xprd[:, ki * LC:(ki + 1) * LC], start=(ki == 0), stop=(ki == DC - 1))
                    sT.inc(_mi)
                    s.wait_ge(sT.h, sT.v)
                    s.activation(sp_scr[:, :], uP[mp][:, :], AF.Exp, bias=bvec[:, mi:mi + 1])
                    sS.inc(s.activation(u_v[:, mi * LC:(mi + 1) * LC], sp_scr[:, :], AF.Ln, bias=1.0))
                    u_evac.append(sS.v)
                    if mi >= 1:
                        t.wait_ge(sS.h, u_evac[mi - 1])
                        t.matmul(dbP[:, :], dbWsb[:, (mi - 1) * 128:mi * 128],
                                 u_v[:, (mi - 1) * LC:mi * LC], start=(mi == 1), stop=False)
                t.wait_ge(sS.h, u_evac[-1])
                sT.inc(t.matmul(dbP[:, :], dbWsb[:, (DC - 1) * 128:DC * 128],
                                u_v[:, (DC - 1) * LC:DC * LC], start=False, stop=True))
                v.wait_ge(sT.h, sT.v)
                sV.inc(v.tensor_copy(dbc_sb[:, :], dbP[:, :]))
                DBC_V = sV.v
                d_evac = []
                for dc in range(DC):
                    mp = dc % 2
                    t.wait_ge(sV.h, DBC_V)
                    if dc >= 2:
                        t.wait_ge(sS.h, d_evac[dc - 2])
                    sT.inc(t.matmul(uP[mp][:, :], dtWsb[64:128, dc * 128:(dc + 1) * 128],
                                    dbc_sb[64:128, :], start=True, stop=True))
                    s.wait_ge(sT.h, sT.v)
                    s.activation(sp_scr[:, :], uP[mp][:, :], AF.Exp, bias=dtb_sb[:, dc:dc + 1])
                    sS.inc(s.activation(d_v[:, dc * LC:(dc + 1) * LC], sp_scr[:, :], AF.Ln, bias=1.0))
                    d_evac.append(sS.v)
                v.wait_ge(sS.h, d_evac[3])
                sV.inc(v.tensor_mul(w_v[:, 0:4 * LC], d_v[:, 0:4 * LC], u_v[:, 0:4 * LC]))
                v.wait_ge(sS.h, d_evac[-1])
                sV.inc(v.tensor_mul(w_v[:, 4 * LC:], d_v[:, 4 * LC:], u_v[:, 4 * LC:]))

                SCAN_V, DA_S, BC_T, REP_S = {}, {}, {}, {}

                def emit_bcast(n_):
                    if n_ >= 2:
                        t.wait_ge(sS.h, REP_S[n_ - 2])
                    _m2 = t.matmul(bcP[n_ % 2][:, 0:LC], ohbc_sb[((n_) % 2) * 32:((n_) % 2) * 32 + 32, ((n_) // 2) * 128:((n_) // 2) * 128 + 128],
                                   dbc_sb[((n_) % 2) * 32:((n_) % 2) * 32 + 32, :], start=True, stop=True)
                    if own:
                        _m2 = t.matmul(bcP[n_ % 2][:, LC:2 * LC], ohbc_sb[((16 + n_) % 2) * 32:((16 + n_) % 2) * 32 + 32, ((16 + n_) // 2) * 128:((16 + n_) // 2) * 128 + 128],
                                       dbc_sb[((16 + n_) % 2) * 32:((16 + n_) % 2) * 32 + 32, :], start=True, stop=True)
                    sT.inc(_m2); BC_T[n_] = sT.v

                def emit_rep(n_):
                    if n_ >= 2:
                        s.wait_ge(sV.h, SCAN_V[n_ - 2])
                    s.wait_ge(sT.h, BC_T[n_])
                    br_ = Brep[n_ % 2]
                    s.copy(br_[:, 0:LC], bcP[n_ % 2][:, 0:LC])
                    s.copy(br_[:, LC:2 * LC], br_[:, 0:LC])
                    _s3 = s.copy(br_[:, 2 * LC:4 * LC], br_[:, 0:2 * LC])
                    if own:
                        cr_ = Crep[n_ % 2]
                        s.copy(cr_[:, 0:LC], bcP[n_ % 2][:, LC:2 * LC])
                        s.copy(cr_[:, LC:2 * LC], cr_[:, 0:LC])
                        _s3 = s.copy(cr_[:, 2 * LC:4 * LC], cr_[:, 0:2 * LC])
                    sS.inc(_s3); REP_S[n_] = sS.v

                def emit_dA_half(n_, half):
                    _s2 = None
                    for dc in range(4 * half, 4 * half + 4):
                        q = n_ * 8 + dc
                        if q >= 14:
                            gg = (q - 14) // 8
                            s.wait_ge(sV.h, SCAN_V[gg])
                        _s2 = s.activation(dA_t[q % 14][:, :], d_v[:, dc * LC:(dc + 1) * LC], AF.Exp,
                                           scale=A_sb[:, dc * N + n_: dc * N + n_ + 1])
                    sS.inc(_s2); DA_S[(n_, half)] = sS.v

                hst3 = hst_sb[:, :].rearrange("p (dc n) -> p dc n", n=N)
                HBg3 = HBg[:, :].rearrange("p (j l) -> p j l", l=LC)

                def emit_scan(n_):
                    for half in range(2):
                        d0 = 4 * half
                        if half == 0:
                            v.wait_ge(sS.h, REP_S[n_])
                            v.wait_ge(sS.h, DA_S[(n_, 0)])
                        else:
                            v.wait_ge(sS.h, DA_S[(n_, 1)])
                        v.tensor_tensor(HBg[:, :], w_v[:, d0 * LC:(d0 + 4) * LC],
                                        Brep[n_ % 2][:, :], OP.mult)
                        for j in range(4):
                            dc = d0 + j
                            hb = HBg[:, j * LC:(j + 1) * LC]
                            init = 0.0 if lc == 0 else hst_sb[:, dc * N + n_: dc * N + n_ + 1]
                            v.tensor_tensor_scan(hb, dA_t[(n_ * 8 + dc) % 14][:, :], hb,
                                                 init, OP.mult, OP.add)
                        _v2 = v.tensor_copy(hst3[:, d0:d0 + 4, n_:n_ + 1],
                                            HBg3[:, 0:4, LC - 1:LC])
                        if own:
                            if n_ == 0:
                                _v2 = v.tensor_tensor(y_v[:, d0 * LC:(d0 + 4) * LC], HBg[:, :],
                                                      Crep[n_ % 2][:, :], OP.mult)
                            else:
                                v.tensor_tensor(ytmp[:, :], HBg[:, :], Crep[n_ % 2][:, :], OP.mult)
                                _v2 = v.tensor_add(y_v[:, d0 * LC:(d0 + 4) * LC],
                                                   y_v[:, d0 * LC:(d0 + 4) * LC], ytmp[:, :])
                        sV.inc(_v2)
                    SCAN_V[n_] = sV.v

                emit_dA_half(0, 0); emit_bcast(0); emit_rep(0); emit_dA_half(0, 1)
                emit_bcast(1); emit_rep(1); emit_dA_half(1, 0)
                for n_ in range(N):
                    emit_scan(n_)
                    if n_ + 1 < N:
                        emit_dA_half(n_ + 1, 1)
                    if n_ + 2 < N:
                        emit_bcast(n_ + 2); emit_rep(n_ + 2); emit_dA_half(n_ + 2, 0)
                if lc == NLC // 2 - 1:
                    sV.inc(v.tensor_scalar(hst_sb[:, :], hst_sb[:, :], al_sb[:, 0:1], None, OP.mult))

                if own:
                    loff = (lc - NLC // 2) * LC
                    z_done = []
                    for dc in range(DC):
                        zp = dc % 2
                        s.wait_ge(dX[0].h, XP_RD)
                        s.wait_ge(sV.h, SCAN_V[N - 1])
                        if dc >= 1:
                            s.wait_ge(sV.h, sV.v)
                        sS.inc(s.activation(z_t[zp][:, :], xprd[:, dc * LC:(dc + 1) * LC], AF.Silu))
                        z_done.append(sS.v)
                        v.wait_ge(sS.h, z_done[-1])
                        if dc == 0 and SD_PREV[0] > 0:
                            v.wait_ge(dSd.h, SD_PREV[0])
                        v.scalar_tensor_tensor(y_v[:, dc * LC:(dc + 1) * LC], u_v[:, dc * LC:(dc + 1) * LC],
                                               Dv_sb[:, dc:dc + 1], y_v[:, dc * LC:(dc + 1) * LC],
                                               OP.mult, OP.add)
                        sV.inc(v.tensor_tensor(s_tmp[:, dc * LC:(dc + 1) * LC],
                                               y_v[:, dc * LC:(dc + 1) * LC], z_t[zp][:, :], OP.mult))
                    g.wait_ge(sV.h, sV.v)
                    if dSd.v > 0:
                        g.wait_ge(dSd.h, dSd.v)
                    dma(g, dSd, sd3[:, :, loff:loff + LC], st3d[:, :, :],
                        self_wait=False, accum_op=OP.add)
                    SD_PREV[0] = dSd.v
        g.wait_ge(dSd.h, dSd.v)
    nc.all_engine_barrier()
    eb_sb = sb("eb_sb", [128, E], F32)
    if _dbg:
        if dOp[0].v > 0:
            sy.wait_ge(dOp[0].h, dOp[0].v)
        dma(sy, dOp[0], dbg_hst.ap()[:, :], hst_sb[:, :], self_wait=False)
        sy.wait_ge(dOp[0].h, dOp[0].v)
        if _upto <= 4:
            es0.close(); return nc

    with ExitStack() as ph:
        psb = lambda n_, sh, dt_: ph.enter_context(nc.sbuf_tensor(n_, sh, dt_))
        ct_sb = psb("ct_sb", [128, DC * E], BF16)
        cwf_sb = psb("cwf_sb", [64, T], BF16)
        xnr = [psb(f"xnr{i}", [128, DC * 128], BF16) for i in range(2)]
        aff_sb = psb("aff_sb", [128, E], F32)
        bia_sb = psb("bia_sb", [128, E], F32)
        m8_sb = psb("m8_sb", [128, 8], F32)
        msk_sb = psb("msk_sb", [128, E], F32)
        sg_sb = psb("sg_sb", [128, E], F32)
        wm_sb = psb("wm_sb", [128, E], F32)
        dn_sb = psb("dn_sb", [128, 1], F32)
        dn2_sb = psb("dn2_sb", [128, 1], F32)
        cwt_sb = psb("cwt_sb", [128, E], F32)
        afP = [ph.enter_context(nc.psum_tensor(f"afP{i}", [128, E], F32)) for i in range(2)]
        cwT = [ph.enter_context(nc.psum_tensor(f"cwT{i}", [128, 128], F32)) for i in range(2)]
        if dCg.v > 0:
            g.wait_ge(dCg.h, dCg.v)
        dma(g, dCg, ct_sb[:, :], ctW.ap()[:, :], self_wait=False)
        CT_D = dCg.v
        EB_D = dma(sy, dC, eb_sb[:, :], bass.AP(ebv, 0, [[0, 128], [1, E]]))
        xnd3r = xnd.ap()[:, :].rearrange("(dc p) t -> p dc t", p=128)
        mm_t = []
        for ti in range(16):
            par = ti % 2
            if ti >= 2:
                sy.wait_ge(sT.h, mm_t[ti - 2])
            XLD = dma(sy, dX[par], xnr[par][:, :].rearrange("p (dc t) -> p dc t", dc=DC),
                      xnd3r[:, :, ti * 128:(ti + 1) * 128])
            t.wait_ge(dX[par].h, XLD)
            t.wait_ge(dCg.h, CT_D)
            if ti >= 2:
                t.wait_ge(sV.h, sV.v)
                t.wait_ge(sS.h, sS.v)
            _mi = None
            for dc in range(DC):
                _mi = t.matmul(afP[par][:, :], xnr[par][:, dc * 128:(dc + 1) * 128],
                               ct_sb[:, dc * E:(dc + 1) * E], start=(dc == 0), stop=(dc == DC - 1))
            sT.inc(_mi); mm_t.append(sT.v)
            v.wait_ge(sT.h, sT.v)
            if ti == 0:
                v.wait_ge(dC.h, EB_D)
            v.tensor_copy(aff_sb[:, :], afP[par][:, :])
            v.tensor_add(bia_sb[:, :], aff_sb[:, :], eb_sb[:, :])
            sV.inc(v.max(m8_sb[:, :], bia_sb[:, :]))
            v.wait_ge(sV.h, sV.v)
            sV.inc(v.tensor_scalar(msk_sb[:, :], bia_sb[:, :], m8_sb[:, 7:8], None, OP.is_ge))
            s.wait_ge(sV.h, sV.v)
            sS.inc(s.activation(sg_sb[:, :], aff_sb[:, :], AF.Sigmoid))
            v.wait_ge(sS.h, sS.v)
            v.tensor_mul(wm_sb[:, :], msk_sb[:, :], sg_sb[:, :])
            sV.inc(v.reduce_sum(dn_sb[:, :], wm_sb[:, :], axis=mybir.AxisListType.X))
            v.wait_ge(sV.h, sV.v)
            v.tensor_single_scalar(dn_sb[:, :], dn_sb[:, :], 1e-8, OP.add)
            sV.inc(v.reciprocal(dn2_sb[:, :], dn_sb[:, :]))
            v.wait_ge(sV.h, sV.v)
            sV.inc(v.tensor_single_scalar(cwt_sb[:, :], wm_sb[:, :], dn2_sb[:, 0:1], OP.mult))
            t.wait_ge(sV.h, sV.v)
            sT.inc(t.transpose(cwT[par][0:64, :], cwt_sb[:, 0:64], id_sb[:, :]))
            s.wait_ge(sT.h, sT.v)
            sS.inc(s.copy(cwf_sb[0:64, ti * 128:(ti + 1) * 128], cwT[par][0:64, :]))
        v.wait_ge(sS.h, sS.v)
        sy.wait_ge(sS.h, sS.v)
        dma(sy, dC, cwfd.ap()[:, :], cwf_sb[0:64, :])
        sy.wait_ge(dC.h, dC.v)
        if _dbg:
            cw32 = psb("cw32", [64, T], F32)
            s.wait_ge(sS.h, sS.v)
            sS.inc(s.copy(cw32[0:64, :], cwf_sb[0:64, :]))
            sy.wait_ge(sS.h, sS.v)
            if dOp[0].v > 0:
                sy.wait_ge(dOp[0].h, dOp[0].v)
            dma(sy, dOp[0], dbg_cw.ap()[:, :], cw32[0:64, :], self_wait=False)
            sy.wait_ge(dOp[0].h, dOp[0].v)
    nc.all_engine_barrier()
    if _dbg and _upto <= 5:
        es0.close(); return nc

    C5 = 512
    NC5 = T // C5
    with ExitStack() as ph:
        psb = lambda n_, sh, dt_: ph.enter_context(nc.sbuf_tensor(n_, sh, dt_))
        wu = [psb(f"wu{i}", [128, DC * F], BF16) for i in range(4)]
        wd = [psb(f"wd{i}", [128, 4 * D], BF16) for i in range(4)]
        wshu = psb("wshu", [128, DC * F], BF16)
        wshd = psb("wshd", [128, 4 * D], BF16)
        xnmv = [psb(f"xnmv{i}", [128, DC * C5], BF16) for i in range(2)]
        h_sb = [psb(f"h_sb{i}", [128, 4 * C5], BF16) for i in range(2)]
        hsh_sb = psb("hsh_sb", [128, 4 * C5], BF16)
        cwb_sb = [[psb(f"cwb_sb{i}_{j}", [128, C5], BF16) for j in range(2)] for i in range(2)]
        mt = psb("mt", [128, DC * C5], F32)
        UPr = [ph.enter_context(nc.psum_tensor(f"UPr{i}", [128, C5], F32)) for i in range(2)]
        RTr = [ph.enter_context(nc.psum_tensor(f"RTr{i}", [128, C5], F32)) for i in range(5)]

        if dCg.v > 0:
            g.wait_ge(dCg.h, dCg.v)
        dma(g, dCg, wshu[:, :].rearrange("p (ki f) -> p ki f", ki=DC),
            wsu.ap()[:, :].rearrange("(ki p) f -> p ki f", p=128), self_wait=False)
        dma(g, dCg, wshd[:, :].rearrange("p (fk d) -> p fk d", fk=4),
            wsd.ap()[:, :].rearrange("(fk p) d -> p fk d", p=128), self_wait=False)
        SH_D = dCg.v

        def load_expert(e):
            sl = e % 4
            if dW[sl].v > 0:
                g.wait_ge(dW[sl].h, dW[sl].v)
            dma(g, dW[sl], wu[sl][:, :].rearrange("p (ki f) -> p ki f", ki=DC),
                wup.ap()[e, :, :].rearrange("(ki p) f -> p ki f", p=128), self_wait=False)
            dma(g, dW[sl], wd[sl][:, :].rearrange("p (fk d) -> p fk d", fk=4),
                wdn.ap()[e, :, :].rearrange("(fk p) d -> p fk d", p=128), self_wait=False)
            return dW[sl].v

        wready = {0: load_expert(0), 1: load_expert(1)}
        xnd3r = xnd.ap()[:, :].rearrange("(dc p) t -> p dc t", p=128)
        rtd3 = rtd.ap()[:, :].rearrange("(dc p) t -> p dc t", p=128)

        n_it = 32 * NC5
        tile_T = {}

        def xn_load(it_):
            c5_ = it_ % NC5
            if it_ >= 2:
                sy.wait_ge(sT.h, tile_T[it_ - 2])
            return dma(sy, dX[it_ % 2], xnmv[it_ % 2][:, :].rearrange("p (dc t) -> p dc t", dc=DC),
                       xnd3r[:, :, c5_ * C5:(c5_ + 1) * C5])

        def cwb_load(it_):
            grp_, c5_ = it_ // NC5, it_ % NC5
            ee0, ee1 = 2 * grp_, 2 * grp_ + 1
            if dSp[0].v > 0:
                sy.wait_ge(sV.h, hs_done.get(it_ - 2, 0))
                sy.wait_ge(dSp[0].h, dSp[0].v)
                sy.wait_ge(dSp[1].h, dSp[1].v)
            dma(sy, dSp[0], cwb_sb[0][it_ % 2][:, :],
                bass.AP(cwfd, ee0 * T + c5_ * C5, [[0, 128], [1, C5]]), self_wait=False)
            dma(sy, dSp[1], cwb_sb[1][it_ % 2][:, :],
                bass.AP(cwfd, ee1 * T + c5_ * C5, [[0, 128], [1, C5]]), self_wait=False)
            return (dSp[0].v, dSp[1].v)

        hs_done = {}
        xn_rd = {0: xn_load(0), 1: xn_load(1)}
        cw_rd = {0: cwb_load(0)}
        expert_down_T = {}
        ev_ring = {}
        dma_it = {}
        it = 0
        for grp in range(32):
            e0, e1 = 2 * grp, 2 * grp + 1
            for c5 in range(NC5):
                par = it % 2

                def up_expert(wsrc, hdst, silu_waits_extra=0):
                    sil = []
                    for fc in range(4):
                        up = UPr[fc % 2]
                        if fc >= 2:
                            t.wait_ge(sS.h, sil[fc - 2])
                        _mi = None
                        for ki in range(DC):
                            _mi = t.matmul(up[:, :], wsrc[:, ki * F + fc * 128: ki * F + (fc + 1) * 128],
                                           xnmv[par][:, ki * C5:(ki + 1) * C5],
                                           start=(ki == 0), stop=(ki == DC - 1))
                        sT.inc(_mi)
                        s.wait_ge(sT.h, sT.v)
                        sS.inc(s.activation(hdst[:, fc * C5:(fc + 1) * C5], up[:, :], AF.Silu))
                        sil.append(sS.v)
                    return sil[-1]

                t.wait_ge(dX[par].h, xn_rd[it])
                if it + 1 < n_it and (it + 1) not in cw_rd:
                    cw_rd[it + 1] = cwb_load(it + 1)
                if grp == 0:
                    t.wait_ge(dCg.h, SH_D)
                    SHS = up_expert(wshu, hsh_sb)
                t.wait_ge(dW[e0 % 4].h, wready[e0])
                S0 = up_expert(wu[e0 % 4], h_sb[0])
                v.wait_ge(dSp[0].h, cw_rd[it][0])
                v.wait_ge(sS.h, S0)
                _vi = None
                for fc in range(4):
                    _vi = v.tensor_tensor(h_sb[0][:, fc * C5:(fc + 1) * C5], h_sb[0][:, fc * C5:(fc + 1) * C5],
                                          cwb_sb[0][it % 2][:, :], OP.mult)
                sV.inc(_vi); HS0 = sV.v
                t.wait_ge(dW[e1 % 4].h, wready[e1])
                S1 = up_expert(wu[e1 % 4], h_sb[1])
                v.wait_ge(dSp[1].h, cw_rd[it][1])
                v.wait_ge(sS.h, S1)
                _vi = None
                for fc in range(4):
                    _vi = v.tensor_tensor(h_sb[1][:, fc * C5:(fc + 1) * C5], h_sb[1][:, fc * C5:(fc + 1) * C5],
                                          cwb_sb[1][it % 2][:, :], OP.mult)
                sV.inc(_vi); HS1 = sV.v
                hs_done[it] = sV.v
                t.wait_ge(sV.h, HS0)
                hs1_waited = False
                halfA_ev = None
                for j, dc in enumerate(range(DC)):
                    slot = j % 5
                    if (it, slot) != (0, slot) and slot in ev_ring:
                        eng_sem, cnt = ev_ring[slot]
                        t.wait_ge(eng_sem, cnt)
                    first_srcs = ([(wshd, hsh_sb)] if grp == 0 else []) + [(wd[e0 % 4], h_sb[0]), (wd[e1 % 4], h_sb[1])]
                    _mi = None
                    nsrc = len(first_srcs)
                    for si, (wsrc, hsrc) in enumerate(first_srcs):
                        if hsrc is h_sb[1] and not hs1_waited:
                            t.wait_ge(sV.h, HS1)
                            hs1_waited = True
                        for fk in range(4):
                            _mi = t.matmul(RTr[slot][:, :],
                                           wsrc[:, fk * D + dc * 128: fk * D + (dc + 1) * 128],
                                           hsrc[:, fk * C5:(fk + 1) * C5],
                                           start=(si == 0 and fk == 0), stop=(si == nsrc - 1 and fk == 3))
                    sT.inc(_mi); DN = sT.v
                    if dc % 2 == 0:
                        v.wait_ge(sT.h, DN)
                        if it >= 1:
                            v.wait_ge(dR[it % 2].h, dR[it % 2].v)
                        sV.inc(v.tensor_copy(mt[:, dc * C5:(dc + 1) * C5], RTr[slot][:, :]))
                        ev_ring[slot] = (sV.h, sV.v)
                    else:
                        s.wait_ge(sT.h, DN)
                        if it >= 1:
                            s.wait_ge(dR[it % 2].h, dR[it % 2].v)
                        sS.inc(s.copy(mt[:, dc * C5:(dc + 1) * C5], RTr[slot][:, :]))
                        ev_ring[slot] = (sS.h, sS.v)
                    if dc == 3:
                        halfA_ev = (sV.v, sS.v)
                    expert_down_T[e0] = DN
                    expert_down_T[e1] = DN
                tile_T[it] = sT.v
                if it + 2 < n_it:
                    xn_rd[it + 2] = xn_load(it + 2)
                aop = OP.bypass if grp == 0 else OP.add
                g.wait_ge(sV.h, halfA_ev[0])
                g.wait_ge(sS.h, halfA_ev[1])
                if dR[it % 2].v > 0:
                    g.wait_ge(dR[it % 2].h, dR[it % 2].v)
                dma(g, dR[it % 2], rtd3[:, 0:4, c5 * C5:(c5 + 1) * C5],
                    mt[:, 0:4 * C5].rearrange("p (dc t) -> p dc t", dc=4), self_wait=False, accum_op=aop)
                g.wait_ge(sV.h, sV.v)
                g.wait_ge(sS.h, sS.v)
                dma(g, dR[it % 2], rtd3[:, 4:8, c5 * C5:(c5 + 1) * C5],
                    mt[:, 4 * C5:8 * C5].rearrange("p (dc t) -> p dc t", dc=4), self_wait=False, accum_op=aop)
                dma_it[it] = dR[it % 2].v
                it += 1
                if c5 == 0:
                    for en in (2 * grp + 2, 2 * grp + 3):
                        if en < 64 and en not in wready:
                            g.wait_ge(sT.h, expert_down_T.get(en - 4, 0))
                            wready[en] = load_expert(en)
        sy.wait_ge(dR[0].h, dR[0].v)
        sy.wait_ge(dR[1].h, dR[1].v)
    nc.all_engine_barrier()
    if _dbg and _upto <= 6:
        es0.close(); return nc

    with ExitStack() as ph:
        psb = lambda n_, sh, dt_: ph.enter_context(nc.sbuf_tensor(n_, sh, dt_))
        rrd = [psb(f"rrd{i}", [128, DC * TC], F32) for i in range(2)]
        srd = [psb(f"srd{i}", [128, DC * TC], F32) for i in range(2)]
        g_sb = psb("g2_sb", [128, DC * TC], F32)
        sg_sb = psb("sg2_sb", [128, DC * TC], F32)
        xrd = [psb(f"xrd{i}", [128, D], F32) for i in range(2)]
        ob = [psb(f"ob{i}", [128, D], F32) for i in range(2)]
        EP = [ph.enter_context(nc.psum_tensor(f"EP{i}", [128, D], F32)) for i in range(2)]
        rtd3 = rtd.ap()[:, :].rearrange("(dc p) t -> p dc t", p=128)
        sg3 = sg_sb[:, :].rearrange("p (dc t) -> p dc t", dc=DC)
        sil_s, mul_v = [], []
        for tc in range(NTC):
            par = tc % 2
            if tc >= 2:
                g.wait_ge(sS.h, sil_s[tc - 2])
                g.wait_ge(sV.h, mul_v[tc - 2])
            RLD = dma(g, dGx[par], rrd[par][:, :].rearrange("p (dc t) -> p dc t", dc=DC),
                      rtd3[:, :, tc * TC:(tc + 1) * TC])
            SLD = dma(g, dSg[par], srd[par][:, :].rearrange("p (dc t) -> p dc t", dc=DC),
                      sd3[:, :, tc * TC:(tc + 1) * TC])
            s.wait_ge(dGx[par].h, RLD)
            if tc >= 1:
                s.wait_ge(sV.h, sV.v)
            sS.inc(s.activation(g_sb[:, :], rrd[par][:, :], AF.Silu)); sil_s.append(sS.v)
            v.wait_ge(sS.h, sS.v)
            v.wait_ge(dSg[par].h, SLD)
            v.wait_ge(sT.h, sT.v)
            sV.inc(v.tensor_mul(sg_sb[:, :], srd[par][:, :], g_sb[:, :]))
            mul_v.append(sV.v)
            for tsub in range(2):
                tpar = tsub
                trow = tc * TC + tsub * 128
                XLD = dma(sy, dSp[tpar], xrd[tpar][:, :], x_own.ap()[trow:trow + 128, :])
                t.wait_ge(sV.h, sV.v)
                _mi = None
                for dc in range(DC):
                    _mi = t.transpose(EP[tpar][:, dc * 128:(dc + 1) * 128],
                                      sg3[:, dc, tsub * 128:(tsub + 1) * 128], id_sb[:, :])
                sT.inc(_mi)
                v.wait_ge(sT.h, sT.v)
                v.wait_ge(dSp[tpar].h, XLD)
                if dOp[tpar].v > 0:
                    v.wait_ge(dOp[tpar].h, dOp[tpar].v)
                sV.inc(v.tensor_add(ob[tpar][:, :], xrd[tpar][:, :], EP[tpar][:, :]))
                sy.wait_ge(sV.h, sV.v)
                dma(sy, dOp[tpar], out.ap()[trow:trow + 128, :], ob[tpar][:, :])
        sy.wait_ge(dOp[0].h, dOp[0].v)
        sy.wait_ge(dOp[1].h, dOp[1].v)
    es0.close()
    return nc


def _pk_vec(a):
    return np.ascontiguousarray(np.asarray(a, np.float32).reshape(8, 128).T)


def _pk_km(a_km, kchunks=8):
    K_, M_ = a_km.shape
    return np.ascontiguousarray(a_km.reshape(kchunks, 128, M_).transpose(1, 0, 2).reshape(128, kchunks * M_))


def _ext_dbw(dbw):
    dbwt = dbw.T
    ext = np.concatenate([dbwt[:, 64:96], dbwt[:, 64:96], dbwt[:, 0:64]], axis=1)
    return _pk_km(np.ascontiguousarray(ext))


_prog = None


def kernel(**inputs):
    global _prog
    if _prog is None:
        _prog = build_nc()
    nc = _prog

    f32 = lambda a: np.ascontiguousarray(np.asarray(a, dtype=np.float32))
    x = f32(inputs["L0_attn_x"])
    common = dict(
        pjW=f32(inputs["L0_attn_proj_W"]).T.copy(),
        fcW=f32(inputs["L0_attn_fconv_W"]).T.copy(),
        bcW=f32(inputs["L0_attn_bconv_W"]).T.copy(),
        dbW=_ext_dbw(f32(inputs["L0_attn_deltaBC_W"])),
        dtW=f32(inputs["L0_attn_dt_W"]).T.copy(),
        Alog=_pk_km(f32(inputs["L0_attn_A_log"])),
        Dv=_pk_vec(inputs["L0_attn_D"]),
        lnw=_pk_vec(inputs["L0_attn_ln_w"]), lnb=_pk_vec(inputs["L0_attn_ln_b"]),
        pjb=_pk_vec(inputs["L0_attn_proj_b"]), fcb=_pk_vec(inputs["L0_attn_fconv_b"]),
        bcb=_pk_vec(inputs["L0_attn_bconv_b"]), dtb=_pk_vec(inputs["L0_attn_dt_b"]),
        ctW=_pk_km(f32(inputs["L0_ffn_centroids"]).T.copy()),
        ebv=f32(inputs["L0_ffn_expert_biases"]),
        wsu=f32(inputs["L0_ffn_Ws_up"]), wsd=f32(inputs["L0_ffn_Ws_down"]),
        wup=f32(inputs["L0_ffn_W_up"]), wdn=f32(inputs["L0_ffn_W_down"]),
    )
    in_maps = []
    for c in range(8):
        b, h = c // 2, c % 2
        xo = np.ascontiguousarray(x[b, h * T:(h + 1) * T, :])
        xp = np.ascontiguousarray(x[b, 0:T, :]) if h == 1 else xo
        al = np.full((128, 1), float(h), np.float32)
        in_maps.append(dict(common, x_own=xo, x_pre=xp, alpha=al))

    res = run_bass_kernel_spmd(nc, in_maps, core_ids=list(range(8)))
    outp = np.empty((B, S, D), dtype=np.float32)
    for c in range(8):
        b, h = c // 2, c % 2
        outp[b, h * T:(h + 1) * T, :] = res.results[c]["out"]
    return outp


# version 14
# speedup vs baseline: 1.0306x; 1.0203x over previous
import numpy as np
import ml_dtypes
from contextlib import ExitStack

import concourse.bass as bass
import concourse.mybir as mybir
from concourse.bass_utils import run_bass_kernel_spmd

F32 = mybir.dt.float32
BF16 = mybir.dt.bfloat16
OP = mybir.AluOpType
AF = mybir.ActivationFunctionType
BF = ml_dtypes.bfloat16

B, S, D = 4, 4096, 1024
T = 2048
N = 16
E, F = 64, 512
DC = 8
LC = 512
NLC = (2 * T) // LC
TC = 256
NTC = T // TC
EPS = 1e-5


class Sem:
    def __init__(self, nc, es, name):
        self.h = es.enter_context(nc.semaphore(name))
        self.v = 0

    def inc(self, instr, k=1):
        instr.then_inc(self.h, k)
        self.v += k
        return instr


def build_nc(_upto=99, _dbg=False):
    nc = bass.Bass("TRN2", target_bir_lowering=False, detect_race_conditions=False)
    g, v, t, s, sy = nc.gpsimd, nc.vector, nc.tensor, nc.scalar, nc.sync

    dI = lambda n_, sh: nc.dram_tensor(n_, sh, F32, kind="ExternalInput")
    x_own = dI("x_own", [T, D])
    x_pre = dI("x_pre", [T, D])
    alpha = dI("alpha", [128, 1])
    pjW = dI("pjW", [D, D])
    fcW = dI("fcW", [D, D])
    bcW = dI("bcW", [D, D])
    dbW = dI("dbW", [128, DC * 128])
    dtW = dI("dtW", [64, D])
    Alog = dI("Alog", [128, DC * N])
    Dv = dI("Dv", [128, DC])
    lnw = dI("lnw", [128, DC]); lnb = dI("lnb", [128, DC])
    pjb = dI("pjb", [128, DC]); fcb = dI("fcb", [128, DC]); bcb = dI("bcb", [128, DC]); dtb = dI("dtb", [128, DC])
    ctW = dI("ctW", [128, DC * E])
    ebv = dI("ebv", [E])
    wsu = dI("wsu", [D, F])
    wsd = dI("wsd", [F, D])
    wup = dI("wup", [E, D, F])
    wdn = dI("wdn", [E, F, D])
    out = nc.dram_tensor("out", [T, D], F32, kind="ExternalOutput")

    _ik = "ExternalOutput" if _dbg else "Internal"
    xn2d = nc.dram_tensor("xn2d", [D, 2 * T], BF16, kind=_ik)
    xnd = nc.dram_tensor("xnd", [D, T], BF16, kind=_ik)
    xpd = nc.dram_tensor("xpd", [D, 2 * T], BF16, kind=_ik)
    rtd = nc.dram_tensor("rtd", [D, T], F32, kind=_ik)
    sd = nc.dram_tensor("sd", [128, DC * T], F32, kind=_ik)
    cwfd = nc.dram_tensor("cwfd", [64, T], BF16, kind="Internal")
    if _dbg:
        dbg_sacc = nc.dram_tensor("dbg_sacc", [128, DC * T], F32, kind="ExternalOutput")
        dbg_cw = nc.dram_tensor("dbg_cw", [64, T], F32, kind="ExternalOutput")
        dbg_hst = nc.dram_tensor("dbg_hst", [128, 128], F32, kind="ExternalOutput")
        dbg_st = nc.dram_tensor("dbg_st", [128, 8 * 32], F32, kind="ExternalOutput")
        dbg_st2 = nc.dram_tensor("dbg_st2", [128, 8 * 32], F32, kind="ExternalOutput")
        dbg_lnw = nc.dram_tensor("dbg_lnw", [128, DC], F32, kind="ExternalOutput")

    identI = nc.inline_tensor(np.eye(128, dtype=np.float32), "identI")
    ohbc_np = np.zeros((128, 16 * 128), dtype=np.float32)
    for s_ in range(32):
        base, colb = (s_ % 2) * 32, (s_ // 2) * 128
        ohbc_np[base + s_, colb:colb + 128] = 1.0
    ohbcI = nc.inline_tensor(ohbc_np, "ohbcI")
    ohe_np = np.zeros((64, 64 * 128), dtype=BF)
    for e_ in range(64):
        ohe_np[e_, e_ * 128:(e_ + 1) * 128] = 1.0
    oheI = nc.inline_tensor(ohe_np, "oheI")

    es0 = ExitStack()
    sb = lambda n_, sh, dt_: es0.enter_context(nc.sbuf_tensor(n_, sh, dt_))

    A_sb = sb("A_sb", [128, DC * N], F32)
    Dv_sb = sb("Dv_sb", [128, DC], F32)
    lnw_sb = sb("lnw_sb", [128, DC], F32)
    lnb_sb = sb("lnb_sb", [128, DC], F32)
    pjb_sb = sb("pjb_sb", [128, DC], F32)
    fcb_sb = sb("fcb_sb", [128, DC], F32)
    bcb_sb = sb("bcb_sb", [128, DC], F32)
    dtb_sb = sb("dtb_sb", [128, DC], F32)
    al_sb = sb("al_sb", [128, 1], F32)
    id_sb = sb("id_sb", [128, 128], F32)
    hst_sb = sb("hst_sb", [128, 128], F32)
    dbc_sb = sb("dbc_sb", [128, LC], F32)
    dbWsb = sb("dbWsb", [128, DC * 128], F32)
    dtWsb = sb("dtWsb", [128, D], F32)
    st_sb = sb("st_sb", [128, 8 * 32], F32)
    eps_sb = sb("eps_sb", [128, 1], F32)

    sT = Sem(nc, es0, "sT"); sV = Sem(nc, es0, "sV"); sS = Sem(nc, es0, "sS")
    dC = Sem(nc, es0, "dC")
    dCg = Sem(nc, es0, "dCg")
    dX = [Sem(nc, es0, f"dX{i}") for i in range(2)]
    dSp = [Sem(nc, es0, f"dSp{i}") for i in range(2)]
    dW = [Sem(nc, es0, f"dW{i}") for i in range(4)]
    dR = [Sem(nc, es0, f"dR{i}") for i in range(2)]
    dOp = [Sem(nc, es0, f"dOp{i}") for i in range(2)]
    dSd = Sem(nc, es0, "dSd")
    dGx = [Sem(nc, es0, f"dGx{i}") for i in range(2)]
    dSg = [Sem(nc, es0, f"dSg{i}") for i in range(2)]
    dTx = [Sem(nc, es0, f"dTx{i}") for i in range(2)]

    def dma(eng, sem, out_, in_, self_wait=True, **kw):
        if self_wait and sem.v > 0:
            eng.wait_ge(sem.h, sem.v)
        eng.dma_start(out=out_, in_=in_, **kw).then_inc(sem.h, 16)
        sem.v += 16
        return sem.v

    dma(sy, dC, A_sb[:, :], Alog.ap()[:, :], self_wait=False)
    for dram, tile in [(Dv, Dv_sb), (lnw, lnw_sb), (lnb, lnb_sb), (pjb, pjb_sb),
                       (fcb, fcb_sb), (bcb, bcb_sb), (dtb, dtb_sb)]:
        dma(sy, dC, tile[:, :], dram.ap()[:, :], self_wait=False)
    dma(sy, dC, al_sb[:, :], alpha.ap()[:, :], self_wait=False)
    dma(sy, dC, dbWsb[:, :], dbW.ap()[:, :], self_wait=False)
    dma(sy, dC, dtWsb[64:128, :], dtW.ap()[:, :], self_wait=False)
    dma(sy, dC, id_sb[:, :], identI.ap()[:, :], self_wait=False)
    P0_LOADS = dC.v
    s.wait_ge(dC.h, P0_LOADS)
    sS.inc(s.activation(A_sb[:, :], A_sb[:, :], AF.Exp))
    v.wait_ge(sS.h, sS.v)
    v.tensor_single_scalar(A_sb[:, :], A_sb[:, :], -1.0, OP.mult)
    v.memset(dbc_sb[:, :], 0.0)
    v.memset(eps_sb[:, :], EPS)
    sV.inc(v.engine_nop())
    V_INIT = sV.v

    stv = st_sb[:, :].rearrange("p (k c) -> p k c", k=8)
    ST_S, ST_Q, ST_MU, ST_V, ST_R, ST_RR2, ST_T1, ST_T2 = range(8)
    stc = lambda k, ti: st_sb[:, k * 32 + ti: k * 32 + ti + 1]

    xsrc = lambda ti: (x_pre if ti < 16 else x_own).ap()[(ti % 16) * 128:(ti % 16) * 128 + 128, :]
    sd3 = sd.ap()[:, :].rearrange("p (dc t) -> p dc t", dc=DC)
    with ExitStack() as ph:
        psb = lambda n_, sh, dt_: ph.enter_context(nc.sbuf_tensor(n_, sh, dt_))
        pps = lambda n_, sh: ph.enter_context(nc.psum_tensor(n_, sh, F32))
        xbuf = [psb(f"xbuf{i}", [128, D], F32) for i in range(2)]
        sq_scr = psb("sq_scr", [128, D], F32)
        xn_tm = [psb(f"xn_tm{i}", [128, D], F32) for i in range(2)]
        xr_tm = [psb(f"xr_tm{i}", [128, D], F32) for i in range(2)]
        xn2b = [psb(f"xn2b{i}", [128, D], BF16) for i in range(2)]
        xnbb = [psb(f"xnbb{i}", [128, D], BF16) for i in range(2)]
        sstg = [psb(f"sstg{i}", [128, D], F32) for i in range(2)]
        trA = [pps(f"trA{i}", [128, D]) for i in range(2)]
        trB = [pps(f"trB{i}", [128, D]) for i in range(2)]

        v_tile, s_tile = [], []
        for ti in range(32):
            par = ti % 2
            if ti >= 2:
                g.wait_ge(sV.h, v_tile[ti - 2])
                g.wait_ge(sS.h, s_tile[ti - 2])
            XLD = dma(g, dGx[par], xbuf[par][:, :], xsrc(ti))
            v.wait_ge(dGx[par].h, XLD)
            if ti == 0:
                v.wait_ge(sV.h, V_INIT)
            sV.inc(v.reduce_sum(stc(ST_S, ti), xbuf[par][:, :], axis=mybir.AxisListType.X))
            v_tile.append(sV.v)
            s.wait_ge(dGx[par].h, XLD)
            sS.inc(s.activation(sq_scr[:, :], xbuf[par][:, :], AF.Square, accum_out=stc(ST_Q, ti)))
            s_tile.append(sS.v)
        v.wait_ge(sS.h, sS.v)
        v.tensor_single_scalar(stv[:, ST_MU, :], stv[:, ST_S, :], 1.0 / D, OP.mult)
        v.tensor_single_scalar(stv[:, ST_T1, :], stv[:, ST_Q, :], 1.0 / D, OP.mult)
        v.tensor_mul(stv[:, ST_T2, :], stv[:, ST_MU, :], stv[:, ST_MU, :])
        sV.inc(v.tensor_sub(stv[:, ST_V, :], stv[:, ST_T1, :], stv[:, ST_T2, :]))
        s.wait_ge(sV.h, sV.v)
        sS.inc(s.activation(stv[:, ST_T1, :], stv[:, ST_V, :], AF.Sqrt, bias=eps_sb[:, 0:1]))
        v.wait_ge(sS.h, sS.v)
        sV.inc(v.reciprocal(stv[:, ST_R, :], stv[:, ST_T1, :]))
        v.wait_ge(sV.h, sV.v)
        v.tensor_mul(stv[:, ST_T1, :], stv[:, ST_V, :], stv[:, ST_R, :])
        sV.inc(v.tensor_mul(stv[:, ST_T2, :], stv[:, ST_T1, :], stv[:, ST_R, :]))
        s.wait_ge(sV.h, sV.v)
        sS.inc(s.activation(stv[:, ST_T1, :], stv[:, ST_T2, :], AF.Sqrt, bias=eps_sb[:, 0:1]))
        v.wait_ge(sS.h, sS.v)
        sV.inc(v.reciprocal(stv[:, ST_T2, :], stv[:, ST_T1, :]))
        v.wait_ge(sV.h, sV.v)
        sV.inc(v.tensor_mul(stv[:, ST_RR2, :], stv[:, ST_R, :], stv[:, ST_T2, :]))
        if _dbg:
            sy.wait_ge(sV.h, sV.v)
            dma(sy, dOp[1], dbg_st2.ap()[:, :], st_sb[:, :], self_wait=False)
            sy.wait_ge(dOp[1].h, dOp[1].v)

        xn2d3 = xn2d.ap()[:, :].rearrange("(dc p) t -> p dc t", p=128)
        xnd3 = xnd.ap()[:, :].rearrange("(dc p) t -> p dc t", p=128)
        ev_v, ev_s, sp_done, sg_done, trd = [], [], [], [], []
        for ti in range(32):
            own = ti >= 16
            par = ti % 2
            if ti >= 2:
                s.wait_ge(sV.h, ev_v[ti - 2])
            XLD = dma(s, dTx[par], xbuf[par][:, :], xsrc(ti))
            v.wait_ge(dTx[par].h, XLD)
            if ti >= 2:
                v.wait_ge(sT.h, trd[ti - 2])
            v.tensor_scalar(xn_tm[par][:, :], xbuf[par][:, :], stc(ST_MU, ti), stc(ST_R, ti), OP.subtract, OP.mult)
            sV.inc(v.tensor_single_scalar(xr_tm[par][:, :], xn_tm[par][:, :], stc(ST_T2, ti), OP.mult))
            NORM_V = sV.v
            t.wait_ge(sV.h, NORM_V)
            if ti >= 2:
                t.wait_ge(sS.h, ev_s[ti - 2])
                t.wait_ge(sV.h, ev_v[ti - 2])
            _ti = None
            for dc in range(DC):
                _ti = t.transpose(trA[par][:, dc * 128:(dc + 1) * 128], xr_tm[par][:, dc * 128:(dc + 1) * 128], id_sb[:, :])
            if own:
                for dc in range(DC):
                    _ti = t.transpose(trB[par][:, dc * 128:(dc + 1) * 128], xn_tm[par][:, dc * 128:(dc + 1) * 128], id_sb[:, :])
            sT.inc(_ti); trd.append(sT.v)
            s.wait_ge(sT.h, sT.v)
            if ti >= 2:
                s.wait_ge(dSp[par].h, sp_done[ti - 2])
                s.wait_ge(dSg[par].h, sg_done[ti - 2])
            _si = None
            for dc in range(DC):
                _si = s.activation(xn2b[par][:, dc * 128:(dc + 1) * 128], trA[par][:, dc * 128:(dc + 1) * 128],
                                   AF.Identity, bias=lnb_sb[:, dc:dc + 1], scale=lnw_sb[:, dc:dc + 1])
            if own:
                _si = s.copy(xnbb[par][:, :], trB[par][:, :])
            sS.inc(_si); ev_s.append(sS.v)
            v.wait_ge(sT.h, sT.v)
            if own:
                if len(sg_done) >= 2:
                    v.wait_ge(dSg[par].h, sg_done[-2])
                sV.inc(v.tensor_copy(sstg[par][:, :], trB[par][:, :]))
            else:
                sV.inc(v.engine_nop())
            ev_v.append(sV.v)
            sy.wait_ge(sS.h, ev_s[-1])
            if dSp[par].v > 0:
                sy.wait_ge(dSp[par].h, dSp[par].v)
            dma(sy, dSp[par], xn2d3[:, :, ti * 128:(ti + 1) * 128],
                xn2b[par][:, :].rearrange("p (dc t) -> p dc t", dc=DC), self_wait=False)
            sp_done.append(dSp[par].v)
            if own:
                ti2 = ti - 16
                g.wait_ge(sS.h, ev_s[-1])
                g.wait_ge(sV.h, ev_v[-1])
                if dSg[par].v > 0:
                    g.wait_ge(dSg[par].h, dSg[par].v)
                dma(g, dSg[par], xnd3[:, :, ti2 * 128:(ti2 + 1) * 128],
                    xnbb[par][:, :].rearrange("p (dc t) -> p dc t", dc=DC), self_wait=False)
                dma(g, dSg[par], sd3[:, :, ti2 * 128:(ti2 + 1) * 128],
                    sstg[par][:, :].rearrange("p (dc t) -> p dc t", dc=DC), self_wait=False)
            sg_done.append(dSg[par].v)
        sy.wait_ge(dSp[0].h, dSp[0].v)
        sy.wait_ge(dSp[1].h, dSp[1].v)
        sy.wait_ge(dSg[0].h, dSg[0].v)
        sy.wait_ge(dSg[1].h, dSg[1].v)
        if _dbg:
            dma(sy, dOp[0], dbg_st.ap()[:, :], st_sb[:, :], self_wait=False)
            dma(sy, dOp[0], dbg_lnw.ap()[:, :], lnw_sb[:, :], self_wait=False)
            sy.wait_ge(dOp[0].h, dOp[0].v)
    nc.all_engine_barrier()
    if _dbg and _upto <= 1:
        es0.close(); return nc

    xpd3 = xpd.ap()[:, :].rearrange("(dc p) t -> p dc t", p=128)
    xn2d3r = xn2d.ap()[:, :].rearrange("(ki p) t -> p ki t", p=128)
    with ExitStack() as ph:
        psb = lambda n_, sh, dt_: ph.enter_context(nc.sbuf_tensor(n_, sh, dt_))
        wmat = psb("wmat", [128, DC * D], BF16)
        x2rd = [psb(f"x2rd{i}", [128, DC * LC], BF16) for i in range(2)]
        xpb = [psb(f"xpb{i}", [128, LC], BF16) for i in range(2)]
        pj = [ph.enter_context(nc.psum_tensor(f"pj{i}", [128, LC], F32)) for i in range(2)]
        if dCg.v > 0:
            g.wait_ge(dCg.h, dCg.v)
        for ki in range(DC):
            dma(g, dCg, wmat[:, ki * D:(ki + 1) * D], pjW.ap()[ki * 128:(ki + 1) * 128, :], self_wait=False)
        W_DONE = dCg.v
        ev, st_cnt, lc_mm = [], [], []
        for lc in range(NLC):
            par = lc % 2
            if lc >= 2:
                g.wait_ge(sT.h, lc_mm[lc - 2])
            XLD = dma(g, dGx[par], x2rd[par][:, :].rearrange("p (ki t) -> p ki t", ki=DC),
                      xn2d3r[:, :, lc * LC:(lc + 1) * LC])
            t.wait_ge(dGx[par].h, XLD)
            t.wait_ge(dCg.h, W_DONE)
            for mi in range(DC):
                mp = mi % 2
                if len(ev) >= 2:
                    t.wait_ge(sS.h, ev[-2])
                _mi = None
                for ki in range(DC):
                    _mi = t.matmul(pj[mp][:, :], wmat[:, ki * D + mi * 128: ki * D + (mi + 1) * 128],
                                   x2rd[par][:, ki * LC:(ki + 1) * LC], start=(ki == 0), stop=(ki == DC - 1))
                sT.inc(_mi)
                s.wait_ge(sT.h, sT.v)
                if len(st_cnt) >= 2:
                    s.wait_ge(dSp[mp].h, st_cnt[-2])
                sS.inc(s.activation(xpb[mp][:, :], pj[mp][:, :], AF.Identity, bias=pjb_sb[:, mi:mi + 1]))
                ev.append(sS.v)
                sy.wait_ge(sS.h, sS.v)
                dma(sy, dSp[mp], xpd3[:, mi, lc * LC:(lc + 1) * LC], xpb[mp][:, :])
                st_cnt.append(dSp[mp].v)
            lc_mm.append(sT.v)
        sy.wait_ge(dSp[0].h, dSp[0].v)
        sy.wait_ge(dSp[1].h, dSp[1].v)
    nc.all_engine_barrier()
    if _dbg and _upto <= 2:
        es0.close(); return nc

    with ExitStack() as ph:
        psb = lambda n_, sh, dt_: ph.enter_context(nc.sbuf_tensor(n_, sh, dt_))
        wmat = psb("wmatB", [128, DC * D], BF16)
        xprd = psb("xprd", [128, DC * LC], BF16)
        u_v = psb("u_v", [128, DC * LC], F32)
        d_v = psb("d_v", [128, DC * LC], F32)
        w_v = psb("w_v", [128, DC * LC], F32)
        y_v = psb("y_v", [128, DC * LC], F32)
        ohbc_sb = psb("ohbc_sb", [128, 16 * 128], F32)
        sp_scr = psb("sp_scr", [128, LC], F32)
        z_t = [sp_scr, sp_scr]
        dA_t = [psb(f"dA{i}", [128, LC], F32) for i in range(14)]
        HBg = psb("HBg", [128, 4 * LC], F32)
        Brep = [psb(f"Brep{i}", [128, 4 * LC], F32) for i in range(2)]
        Crep = [psb(f"Crep{i}", [128, 4 * LC], F32) for i in range(2)]
        ytmp = psb("ytmp", [128, 4 * LC], F32)
        s_tmp = psb("s_tmp", [128, DC * LC], F32)
        uP = [ph.enter_context(nc.psum_tensor(f"uP{i}", [128, LC], F32)) for i in range(2)]
        dbP = ph.enter_context(nc.psum_tensor("dbP", [128, LC], F32))
        bcP = [ph.enter_context(nc.psum_tensor(f"bcP{i}", [128, 2 * LC], F32)) for i in range(2)]

        if dCg.v > 0:
            g.wait_ge(dCg.h, dCg.v)
        dma(g, dCg, ohbc_sb[:, :], ohbcI.ap()[:, :], self_wait=False)
        xpd3r = xpd.ap()[:, :].rearrange("(dc p) t -> p dc t", p=128)
        st3d = s_tmp[:, :].rearrange("p (dc t) -> p dc t", dc=DC)
        SD_PREV = [0]

        for br, (Wd, bvec) in enumerate([(fcW, fcb_sb), (bcW, bcb_sb)]):
            g.wait_ge(sT.h, sT.v)
            g.wait_ge(dCg.h, dCg.v)
            for ki in range(DC):
                dma(g, dCg, wmat[:, ki * D:(ki + 1) * D], Wd.ap()[ki * 128:(ki + 1) * 128, :], self_wait=False)
            WBR = dCg.v
            for lc in range(NLC):
                own = lc >= NLC // 2
                sy.wait_ge(sV.h, sV.v)
                sy.wait_ge(sS.h, sS.v)
                XP_RD = dma(sy, dX[0], xprd[:, :].rearrange("p (ki t) -> p ki t", ki=DC),
                            xpd3r[:, :, lc * LC:(lc + 1) * LC])
                t.wait_ge(dX[0].h, XP_RD)
                t.wait_ge(dCg.h, WBR)
                s.wait_ge(sV.h, sV.v)
                u_evac = []
                for mi in range(DC):
                    mp = mi % 2
                    _mi = None
                    for ki in range(DC):
                        _mi = t.matmul(uP[mp][:, :], wmat[:, ki * D + mi * 128: ki * D + (mi + 1) * 128],
                                       xprd[:, ki * LC:(ki + 1) * LC], start=(ki == 0), stop=(ki == DC - 1))
                    sT.inc(_mi)
                    s.wait_ge(sT.h, sT.v)
                    s.activation(sp_scr[:, :], uP[mp][:, :], AF.Exp, bias=bvec[:, mi:mi + 1])
                    sS.inc(s.activation(u_v[:, mi * LC:(mi + 1) * LC], sp_scr[:, :], AF.Ln, bias=1.0))
                    u_evac.append(sS.v)
                    if mi >= 1:
                        t.wait_ge(sS.h, u_evac[mi - 1])
                        t.matmul(dbP[:, :], dbWsb[:, (mi - 1) * 128:mi * 128],
                                 u_v[:, (mi - 1) * LC:mi * LC], start=(mi == 1), stop=False)
                t.wait_ge(sS.h, u_evac[-1])
                sT.inc(t.matmul(dbP[:, :], dbWsb[:, (DC - 1) * 128:DC * 128],
                                u_v[:, (DC - 1) * LC:DC * LC], start=False, stop=True))
                v.wait_ge(sT.h, sT.v)
                sV.inc(v.tensor_copy(dbc_sb[:, :], dbP[:, :]))
                DBC_V = sV.v
                d_evac = []
                for dc in range(DC):
                    mp = dc % 2
                    t.wait_ge(sV.h, DBC_V)
                    if dc >= 2:
                        t.wait_ge(sS.h, d_evac[dc - 2])
                    sT.inc(t.matmul(uP[mp][:, :], dtWsb[64:128, dc * 128:(dc + 1) * 128],
                                    dbc_sb[64:128, :], start=True, stop=True))
                    s.wait_ge(sT.h, sT.v)
                    s.activation(sp_scr[:, :], uP[mp][:, :], AF.Exp, bias=dtb_sb[:, dc:dc + 1])
                    sS.inc(s.activation(d_v[:, dc * LC:(dc + 1) * LC], sp_scr[:, :], AF.Ln, bias=1.0))
                    d_evac.append(sS.v)
                v.wait_ge(sS.h, d_evac[3])
                sV.inc(v.tensor_mul(w_v[:, 0:4 * LC], d_v[:, 0:4 * LC], u_v[:, 0:4 * LC]))
                v.wait_ge(sS.h, d_evac[-1])
                sV.inc(v.tensor_mul(w_v[:, 4 * LC:], d_v[:, 4 * LC:], u_v[:, 4 * LC:]))

                SCAN_V, DA_S, BC_T, REP_S = {}, {}, {}, {}

                def emit_bcast(n_):
                    if n_ >= 2:
                        t.wait_ge(sS.h, REP_S[n_ - 2])
                    _m2 = t.matmul(bcP[n_ % 2][:, 0:LC], ohbc_sb[((n_) % 2) * 32:((n_) % 2) * 32 + 32, ((n_) // 2) * 128:((n_) // 2) * 128 + 128],
                                   dbc_sb[((n_) % 2) * 32:((n_) % 2) * 32 + 32, :], start=True, stop=True)
                    if own:
                        _m2 = t.matmul(bcP[n_ % 2][:, LC:2 * LC], ohbc_sb[((16 + n_) % 2) * 32:((16 + n_) % 2) * 32 + 32, ((16 + n_) // 2) * 128:((16 + n_) // 2) * 128 + 128],
                                       dbc_sb[((16 + n_) % 2) * 32:((16 + n_) % 2) * 32 + 32, :], start=True, stop=True)
                    sT.inc(_m2); BC_T[n_] = sT.v

                def emit_rep(n_):
                    if n_ >= 2:
                        s.wait_ge(sV.h, SCAN_V[n_ - 2])
                    s.wait_ge(sT.h, BC_T[n_])
                    br_ = Brep[n_ % 2]
                    s.copy(br_[:, 0:LC], bcP[n_ % 2][:, 0:LC])
                    s.copy(br_[:, LC:2 * LC], br_[:, 0:LC])
                    _s3 = s.copy(br_[:, 2 * LC:4 * LC], br_[:, 0:2 * LC])
                    if own:
                        cr_ = Crep[n_ % 2]
                        s.copy(cr_[:, 0:LC], bcP[n_ % 2][:, LC:2 * LC])
                        s.copy(cr_[:, LC:2 * LC], cr_[:, 0:LC])
                        _s3 = s.copy(cr_[:, 2 * LC:4 * LC], cr_[:, 0:2 * LC])
                    sS.inc(_s3); REP_S[n_] = sS.v

                def emit_dA_half(n_, half):
                    _s2 = None
                    for dc in range(4 * half, 4 * half + 4):
                        q = n_ * 8 + dc
                        if q >= 14:
                            gg = (q - 14) // 8
                            s.wait_ge(sV.h, SCAN_V[gg])
                        _s2 = s.activation(dA_t[q % 14][:, :], d_v[:, dc * LC:(dc + 1) * LC], AF.Exp,
                                           scale=A_sb[:, dc * N + n_: dc * N + n_ + 1])
                    sS.inc(_s2); DA_S[(n_, half)] = sS.v

                hst3 = hst_sb[:, :].rearrange("p (dc n) -> p dc n", n=N)
                HBg3 = HBg[:, :].rearrange("p (j l) -> p j l", l=LC)

                def emit_scan(n_):
                    for half in range(2):
                        d0 = 4 * half
                        if half == 0:
                            v.wait_ge(sS.h, REP_S[n_])
                            v.wait_ge(sS.h, DA_S[(n_, 0)])
                        else:
                            v.wait_ge(sS.h, DA_S[(n_, 1)])
                        v.tensor_tensor(HBg[:, :], w_v[:, d0 * LC:(d0 + 4) * LC],
                                        Brep[n_ % 2][:, :], OP.mult)
                        for j in range(4):
                            dc = d0 + j
                            hb = HBg[:, j * LC:(j + 1) * LC]
                            init = 0.0 if lc == 0 else hst_sb[:, dc * N + n_: dc * N + n_ + 1]
                            v.tensor_tensor_scan(hb, dA_t[(n_ * 8 + dc) % 14][:, :], hb,
                                                 init, OP.mult, OP.add)
                        _v2 = v.tensor_copy(hst3[:, d0:d0 + 4, n_:n_ + 1],
                                            HBg3[:, 0:4, LC - 1:LC])
                        if own:
                            if n_ == 0:
                                _v2 = v.tensor_tensor(y_v[:, d0 * LC:(d0 + 4) * LC], HBg[:, :],
                                                      Crep[n_ % 2][:, :], OP.mult)
                            else:
                                v.tensor_tensor(ytmp[:, :], HBg[:, :], Crep[n_ % 2][:, :], OP.mult)
                                _v2 = v.tensor_add(y_v[:, d0 * LC:(d0 + 4) * LC],
                                                   y_v[:, d0 * LC:(d0 + 4) * LC], ytmp[:, :])
                        sV.inc(_v2)
                    SCAN_V[n_] = sV.v

                emit_dA_half(0, 0); emit_bcast(0); emit_rep(0); emit_dA_half(0, 1)
                emit_bcast(1); emit_rep(1); emit_dA_half(1, 0)
                for n_ in range(N):
                    emit_scan(n_)
                    if n_ + 1 < N:
                        emit_dA_half(n_ + 1, 1)
                    if n_ + 2 < N:
                        emit_bcast(n_ + 2); emit_rep(n_ + 2); emit_dA_half(n_ + 2, 0)
                if lc == NLC // 2 - 1:
                    sV.inc(v.tensor_scalar(hst_sb[:, :], hst_sb[:, :], al_sb[:, 0:1], None, OP.mult))

                if own:
                    loff = (lc - NLC // 2) * LC
                    z_done = []
                    for dc in range(DC):
                        zp = dc % 2
                        s.wait_ge(dX[0].h, XP_RD)
                        s.wait_ge(sV.h, SCAN_V[N - 1])
                        if dc >= 1:
                            s.wait_ge(sV.h, sV.v)
                        sS.inc(s.activation(z_t[zp][:, :], xprd[:, dc * LC:(dc + 1) * LC], AF.Silu))
                        z_done.append(sS.v)
                        v.wait_ge(sS.h, z_done[-1])
                        if dc == 0 and SD_PREV[0] > 0:
                            v.wait_ge(dSd.h, SD_PREV[0])
                        v.scalar_tensor_tensor(y_v[:, dc * LC:(dc + 1) * LC], u_v[:, dc * LC:(dc + 1) * LC],
                                               Dv_sb[:, dc:dc + 1], y_v[:, dc * LC:(dc + 1) * LC],
                                               OP.mult, OP.add)
                        sV.inc(v.tensor_tensor(s_tmp[:, dc * LC:(dc + 1) * LC],
                                               y_v[:, dc * LC:(dc + 1) * LC], z_t[zp][:, :], OP.mult))
                    g.wait_ge(sV.h, sV.v)
                    if dSd.v > 0:
                        g.wait_ge(dSd.h, dSd.v)
                    dma(g, dSd, sd3[:, :, loff:loff + LC], st3d[:, :, :],
                        self_wait=False, accum_op=OP.add)
                    SD_PREV[0] = dSd.v
        g.wait_ge(dSd.h, dSd.v)
    nc.all_engine_barrier()
    eb_sb = sb("eb_sb", [128, E], F32)
    if _dbg:
        if dOp[0].v > 0:
            sy.wait_ge(dOp[0].h, dOp[0].v)
        dma(sy, dOp[0], dbg_hst.ap()[:, :], hst_sb[:, :], self_wait=False)
        sy.wait_ge(dOp[0].h, dOp[0].v)
        if _upto <= 4:
            es0.close(); return nc

    with ExitStack() as ph:
        psb = lambda n_, sh, dt_: ph.enter_context(nc.sbuf_tensor(n_, sh, dt_))
        ct_sb = psb("ct_sb", [128, DC * E], BF16)
        cwf_sb = psb("cwf_sb", [64, T], BF16)
        xnr = [psb(f"xnr{i}", [128, DC * 128], BF16) for i in range(2)]
        aff_sb = psb("aff_sb", [128, E], F32)
        bia_sb = psb("bia_sb", [128, E], F32)
        m8_sb = psb("m8_sb", [128, 8], F32)
        msk_sb = psb("msk_sb", [128, E], F32)
        sg_sb = psb("sg_sb", [128, E], F32)
        wm_sb = psb("wm_sb", [128, E], F32)
        dn_sb = psb("dn_sb", [128, 1], F32)
        dn2_sb = psb("dn2_sb", [128, 1], F32)
        cwt_sb = psb("cwt_sb", [128, E], F32)
        afP = [ph.enter_context(nc.psum_tensor(f"afP{i}", [128, E], F32)) for i in range(2)]
        cwT = [ph.enter_context(nc.psum_tensor(f"cwT{i}", [128, 128], F32)) for i in range(2)]
        if dCg.v > 0:
            g.wait_ge(dCg.h, dCg.v)
        dma(g, dCg, ct_sb[:, :], ctW.ap()[:, :], self_wait=False)
        CT_D = dCg.v
        EB_D = dma(sy, dC, eb_sb[:, :], bass.AP(ebv, 0, [[0, 128], [1, E]]))
        xnd3r = xnd.ap()[:, :].rearrange("(dc p) t -> p dc t", p=128)
        mm_t = []
        for ti in range(16):
            par = ti % 2
            if ti >= 2:
                sy.wait_ge(sT.h, mm_t[ti - 2])
            XLD = dma(sy, dX[par], xnr[par][:, :].rearrange("p (dc t) -> p dc t", dc=DC),
                      xnd3r[:, :, ti * 128:(ti + 1) * 128])
            t.wait_ge(dX[par].h, XLD)
            t.wait_ge(dCg.h, CT_D)
            if ti >= 2:
                t.wait_ge(sV.h, sV.v)
                t.wait_ge(sS.h, sS.v)
            _mi = None
            for dc in range(DC):
                _mi = t.matmul(afP[par][:, :], xnr[par][:, dc * 128:(dc + 1) * 128],
                               ct_sb[:, dc * E:(dc + 1) * E], start=(dc == 0), stop=(dc == DC - 1))
            sT.inc(_mi); mm_t.append(sT.v)
            v.wait_ge(sT.h, sT.v)
            if ti == 0:
                v.wait_ge(dC.h, EB_D)
            v.tensor_copy(aff_sb[:, :], afP[par][:, :])
            v.tensor_add(bia_sb[:, :], aff_sb[:, :], eb_sb[:, :])
            sV.inc(v.max(m8_sb[:, :], bia_sb[:, :]))
            v.wait_ge(sV.h, sV.v)
            sV.inc(v.tensor_scalar(msk_sb[:, :], bia_sb[:, :], m8_sb[:, 7:8], None, OP.is_ge))
            s.wait_ge(sV.h, sV.v)
            sS.inc(s.activation(sg_sb[:, :], aff_sb[:, :], AF.Sigmoid))
            v.wait_ge(sS.h, sS.v)
            v.tensor_mul(wm_sb[:, :], msk_sb[:, :], sg_sb[:, :])
            sV.inc(v.reduce_sum(dn_sb[:, :], wm_sb[:, :], axis=mybir.AxisListType.X))
            v.wait_ge(sV.h, sV.v)
            v.tensor_single_scalar(dn_sb[:, :], dn_sb[:, :], 1e-8, OP.add)
            sV.inc(v.reciprocal(dn2_sb[:, :], dn_sb[:, :]))
            v.wait_ge(sV.h, sV.v)
            sV.inc(v.tensor_single_scalar(cwt_sb[:, :], wm_sb[:, :], dn2_sb[:, 0:1], OP.mult))
            t.wait_ge(sV.h, sV.v)
            sT.inc(t.transpose(cwT[par][0:64, :], cwt_sb[:, 0:64], id_sb[:, :]))
            s.wait_ge(sT.h, sT.v)
            sS.inc(s.copy(cwf_sb[0:64, ti * 128:(ti + 1) * 128], cwT[par][0:64, :]))
        v.wait_ge(sS.h, sS.v)
        sy.wait_ge(sS.h, sS.v)
        dma(sy, dC, cwfd.ap()[:, :], cwf_sb[0:64, :])
        sy.wait_ge(dC.h, dC.v)
        if _dbg:
            cw32 = psb("cw32", [64, T], F32)
            s.wait_ge(sS.h, sS.v)
            sS.inc(s.copy(cw32[0:64, :], cwf_sb[0:64, :]))
            sy.wait_ge(sS.h, sS.v)
            if dOp[0].v > 0:
                sy.wait_ge(dOp[0].h, dOp[0].v)
            dma(sy, dOp[0], dbg_cw.ap()[:, :], cw32[0:64, :], self_wait=False)
            sy.wait_ge(dOp[0].h, dOp[0].v)
    nc.all_engine_barrier()
    if _dbg and _upto <= 5:
        es0.close(); return nc

    C5 = 512
    NC5 = T // C5
    with ExitStack() as ph:
        psb = lambda n_, sh, dt_: ph.enter_context(nc.sbuf_tensor(n_, sh, dt_))
        wu = [psb(f"wu{i}", [128, DC * F], BF16) for i in range(4)]
        wd = [psb(f"wd{i}", [128, 4 * D], BF16) for i in range(4)]
        wshu = psb("wshu", [128, DC * F], BF16)
        wshd = psb("wshd", [128, 4 * D], BF16)
        xnmv = [psb(f"xnmv{i}", [128, DC * C5], BF16) for i in range(2)]
        h_sb = [psb(f"h_sb{i}", [128, 4 * C5], BF16) for i in range(2)]
        hsh_sb = psb("hsh_sb", [128, 4 * C5], BF16)
        cwb_sb = [[psb(f"cwb_sb{i}_{j}", [128, C5], BF16) for j in range(2)] for i in range(2)]
        mt = psb("mt", [128, DC * C5], F32)
        UPr = [ph.enter_context(nc.psum_tensor(f"UPr{i}", [128, C5], F32)) for i in range(2)]
        RTr = [ph.enter_context(nc.psum_tensor(f"RTr{i}", [128, C5], F32)) for i in range(5)]

        if dCg.v > 0:
            g.wait_ge(dCg.h, dCg.v)
        dma(g, dCg, wshu[:, :].rearrange("p (ki f) -> p ki f", ki=DC),
            wsu.ap()[:, :].rearrange("(ki p) f -> p ki f", p=128), self_wait=False)
        dma(g, dCg, wshd[:, :].rearrange("p (fk d) -> p fk d", fk=4),
            wsd.ap()[:, :].rearrange("(fk p) d -> p fk d", p=128), self_wait=False)
        SH_D = dCg.v

        def load_expert(e):
            sl = e % 4
            if dW[sl].v > 0:
                g.wait_ge(dW[sl].h, dW[sl].v)
            dma(g, dW[sl], wu[sl][:, :].rearrange("p (ki f) -> p ki f", ki=DC),
                wup.ap()[e, :, :].rearrange("(ki p) f -> p ki f", p=128), self_wait=False)
            dma(g, dW[sl], wd[sl][:, :].rearrange("p (fk d) -> p fk d", fk=4),
                wdn.ap()[e, :, :].rearrange("(fk p) d -> p fk d", p=128), self_wait=False)
            return dW[sl].v

        wready = {0: load_expert(0), 1: load_expert(1)}
        xnd3r = xnd.ap()[:, :].rearrange("(dc p) t -> p dc t", p=128)
        rtd3 = rtd.ap()[:, :].rearrange("(dc p) t -> p dc t", p=128)

        n_it = 32 * NC5
        tile_T = {}

        def xn_load(it_):
            c5_ = it_ % NC5
            if it_ >= 2:
                sy.wait_ge(sT.h, tile_T[it_ - 2])
            return dma(sy, dX[it_ % 2], xnmv[it_ % 2][:, :].rearrange("p (dc t) -> p dc t", dc=DC),
                       xnd3r[:, :, c5_ * C5:(c5_ + 1) * C5])

        def cwb_load(it_):
            grp_, c5_ = it_ // NC5, it_ % NC5
            ee0, ee1 = 2 * grp_, 2 * grp_ + 1
            if dSp[0].v > 0:
                sy.wait_ge(sV.h, hs_done.get(it_ - 2, 0))
                sy.wait_ge(dSp[0].h, dSp[0].v)
                sy.wait_ge(dSp[1].h, dSp[1].v)
            dma(sy, dSp[0], cwb_sb[0][it_ % 2][:, :],
                bass.AP(cwfd, ee0 * T + c5_ * C5, [[0, 128], [1, C5]]), self_wait=False)
            dma(sy, dSp[1], cwb_sb[1][it_ % 2][:, :],
                bass.AP(cwfd, ee1 * T + c5_ * C5, [[0, 128], [1, C5]]), self_wait=False)
            return (dSp[0].v, dSp[1].v)

        hs_done = {}
        xn_rd = {0: xn_load(0), 1: xn_load(1)}
        cw_rd = {0: cwb_load(0)}
        expert_down_T = {}
        ev_ring = {}
        dma_it = {}
        it = 0
        for grp in range(32):
            e0, e1 = 2 * grp, 2 * grp + 1
            for c5 in range(NC5):
                par = it % 2

                def up_expert(wsrc, hdst, silu_waits_extra=0):
                    sil = []
                    for fc in range(4):
                        up = UPr[fc % 2]
                        if fc >= 2:
                            t.wait_ge(sS.h, sil[fc - 2])
                        _mi = None
                        for ki in range(DC):
                            _mi = t.matmul(up[:, :], wsrc[:, ki * F + fc * 128: ki * F + (fc + 1) * 128],
                                           xnmv[par][:, ki * C5:(ki + 1) * C5],
                                           start=(ki == 0), stop=(ki == DC - 1))
                        sT.inc(_mi)
                        s.wait_ge(sT.h, sT.v)
                        sS.inc(s.activation(hdst[:, fc * C5:(fc + 1) * C5], up[:, :], AF.Silu))
                        sil.append(sS.v)
                    return sil[-1]

                t.wait_ge(dX[par].h, xn_rd[it])
                if it + 1 < n_it and (it + 1) not in cw_rd:
                    cw_rd[it + 1] = cwb_load(it + 1)
                if grp == 0:
                    t.wait_ge(dCg.h, SH_D)
                    SHS = up_expert(wshu, hsh_sb)
                t.wait_ge(dW[e0 % 4].h, wready[e0])
                S0 = up_expert(wu[e0 % 4], h_sb[0])
                v.wait_ge(dSp[0].h, cw_rd[it][0])
                v.wait_ge(sS.h, S0)
                _vi = None
                for fc in range(4):
                    _vi = v.tensor_tensor(h_sb[0][:, fc * C5:(fc + 1) * C5], h_sb[0][:, fc * C5:(fc + 1) * C5],
                                          cwb_sb[0][it % 2][:, :], OP.mult)
                sV.inc(_vi); HS0 = sV.v
                t.wait_ge(dW[e1 % 4].h, wready[e1])
                S1 = up_expert(wu[e1 % 4], h_sb[1])
                v.wait_ge(dSp[1].h, cw_rd[it][1])
                v.wait_ge(sS.h, S1)
                _vi = None
                for fc in range(4):
                    _vi = v.tensor_tensor(h_sb[1][:, fc * C5:(fc + 1) * C5], h_sb[1][:, fc * C5:(fc + 1) * C5],
                                          cwb_sb[1][it % 2][:, :], OP.mult)
                sV.inc(_vi); HS1 = sV.v
                hs_done[it] = sV.v
                t.wait_ge(sV.h, HS0)
                hs1_waited = False
                halfA_ev = None
                for j, dc in enumerate(range(DC)):
                    slot = j % 5
                    if (it, slot) != (0, slot) and slot in ev_ring:
                        eng_sem, cnt = ev_ring[slot]
                        t.wait_ge(eng_sem, cnt)
                    first_srcs = ([(wshd, hsh_sb)] if grp == 0 else []) + [(wd[e0 % 4], h_sb[0]), (wd[e1 % 4], h_sb[1])]
                    _mi = None
                    nsrc = len(first_srcs)
                    for si, (wsrc, hsrc) in enumerate(first_srcs):
                        if hsrc is h_sb[1] and not hs1_waited:
                            t.wait_ge(sV.h, HS1)
                            hs1_waited = True
                        for fk in range(4):
                            _mi = t.matmul(RTr[slot][:, :],
                                           wsrc[:, fk * D + dc * 128: fk * D + (dc + 1) * 128],
                                           hsrc[:, fk * C5:(fk + 1) * C5],
                                           start=(si == 0 and fk == 0), stop=(si == nsrc - 1 and fk == 3))
                    sT.inc(_mi); DN = sT.v
                    if dc % 2 == 0:
                        v.wait_ge(sT.h, DN)
                        if it >= 1:
                            v.wait_ge(dR[it % 2].h, dR[it % 2].v)
                        sV.inc(v.tensor_copy(mt[:, dc * C5:(dc + 1) * C5], RTr[slot][:, :]))
                        ev_ring[slot] = (sV.h, sV.v)
                    else:
                        s.wait_ge(sT.h, DN)
                        if it >= 1:
                            s.wait_ge(dR[it % 2].h, dR[it % 2].v)
                        sS.inc(s.copy(mt[:, dc * C5:(dc + 1) * C5], RTr[slot][:, :]))
                        ev_ring[slot] = (sS.h, sS.v)
                    if dc == 3:
                        halfA_ev = (sV.v, sS.v)
                    expert_down_T[e0] = DN
                    expert_down_T[e1] = DN
                tile_T[it] = sT.v
                if it + 2 < n_it:
                    xn_rd[it + 2] = xn_load(it + 2)
                aop = OP.bypass if grp == 0 else OP.add
                g.wait_ge(sV.h, halfA_ev[0])
                g.wait_ge(sS.h, halfA_ev[1])
                if dR[it % 2].v > 0:
                    g.wait_ge(dR[it % 2].h, dR[it % 2].v)
                dma(g, dR[it % 2], rtd3[:, 0:4, c5 * C5:(c5 + 1) * C5],
                    mt[:, 0:4 * C5].rearrange("p (dc t) -> p dc t", dc=4), self_wait=False, accum_op=aop)
                g.wait_ge(sV.h, sV.v)
                g.wait_ge(sS.h, sS.v)
                dma(g, dR[it % 2], rtd3[:, 4:8, c5 * C5:(c5 + 1) * C5],
                    mt[:, 4 * C5:8 * C5].rearrange("p (dc t) -> p dc t", dc=4), self_wait=False, accum_op=aop)
                dma_it[it] = dR[it % 2].v
                it += 1
                if c5 == 0:
                    for en in (2 * grp + 2, 2 * grp + 3):
                        if en < 64 and en not in wready:
                            g.wait_ge(sT.h, expert_down_T.get(en - 4, 0))
                            wready[en] = load_expert(en)
        sy.wait_ge(dR[0].h, dR[0].v)
        sy.wait_ge(dR[1].h, dR[1].v)
    nc.all_engine_barrier()
    if _dbg and _upto <= 6:
        es0.close(); return nc

    with ExitStack() as ph:
        psb = lambda n_, sh, dt_: ph.enter_context(nc.sbuf_tensor(n_, sh, dt_))
        rrd = [psb(f"rrd{i}", [128, DC * TC], F32) for i in range(2)]
        srd = [psb(f"srd{i}", [128, DC * TC], F32) for i in range(2)]
        g_sb = psb("g2_sb", [128, DC * TC], F32)
        sg_sb = psb("sg2_sb", [128, DC * TC], F32)
        xrd = [psb(f"xrd{i}", [128, D], F32) for i in range(2)]
        ob = [psb(f"ob{i}", [128, D], F32) for i in range(2)]
        EP = [ph.enter_context(nc.psum_tensor(f"EP{i}", [128, D], F32)) for i in range(2)]
        rtd3 = rtd.ap()[:, :].rearrange("(dc p) t -> p dc t", p=128)
        sg3 = sg_sb[:, :].rearrange("p (dc t) -> p dc t", dc=DC)
        sil_s, mul_v = [], []
        for tc in range(NTC):
            par = tc % 2
            if tc >= 2:
                g.wait_ge(sS.h, sil_s[tc - 2])
                g.wait_ge(sV.h, mul_v[tc - 2])
            RLD = dma(g, dGx[par], rrd[par][:, :].rearrange("p (dc t) -> p dc t", dc=DC),
                      rtd3[:, :, tc * TC:(tc + 1) * TC])
            SLD = dma(g, dSg[par], srd[par][:, :].rearrange("p (dc t) -> p dc t", dc=DC),
                      sd3[:, :, tc * TC:(tc + 1) * TC])
            s.wait_ge(dGx[par].h, RLD)
            if tc >= 1:
                s.wait_ge(sV.h, sV.v)
            sS.inc(s.activation(g_sb[:, :], rrd[par][:, :], AF.Silu)); sil_s.append(sS.v)
            v.wait_ge(sS.h, sS.v)
            v.wait_ge(dSg[par].h, SLD)
            v.wait_ge(sT.h, sT.v)
            sV.inc(v.tensor_mul(sg_sb[:, :], srd[par][:, :], g_sb[:, :]))
            mul_v.append(sV.v)
            for tsub in range(2):
                tpar = tsub
                trow = tc * TC + tsub * 128
                XLD = dma(sy, dSp[tpar], xrd[tpar][:, :], x_own.ap()[trow:trow + 128, :])
                t.wait_ge(sV.h, sV.v)
                _mi = None
                for dc in range(DC):
                    _mi = t.transpose(EP[tpar][:, dc * 128:(dc + 1) * 128],
                                      sg3[:, dc, tsub * 128:(tsub + 1) * 128], id_sb[:, :])
                sT.inc(_mi)
                v.wait_ge(sT.h, sT.v)
                v.wait_ge(dSp[tpar].h, XLD)
                if dOp[tpar].v > 0:
                    v.wait_ge(dOp[tpar].h, dOp[tpar].v)
                sV.inc(v.tensor_add(ob[tpar][:, :], xrd[tpar][:, :], EP[tpar][:, :]))
                sy.wait_ge(sV.h, sV.v)
                dma(sy, dOp[tpar], out.ap()[trow:trow + 128, :], ob[tpar][:, :])
        sy.wait_ge(dOp[0].h, dOp[0].v)
        sy.wait_ge(dOp[1].h, dOp[1].v)
    es0.close()
    return nc


def _pk_vec(a):
    return np.ascontiguousarray(np.asarray(a, np.float32).reshape(8, 128).T)


def _pk_km(a_km, kchunks=8):
    K_, M_ = a_km.shape
    return np.ascontiguousarray(a_km.reshape(kchunks, 128, M_).transpose(1, 0, 2).reshape(128, kchunks * M_))


def _ext_dbw(dbw):
    dbwt = dbw.T
    ext = np.concatenate([dbwt[:, 64:96], dbwt[:, 64:96], dbwt[:, 0:64]], axis=1)
    return _pk_km(np.ascontiguousarray(ext))


_prog = None


def kernel(**inputs):
    global _prog
    if _prog is None:
        _prog = build_nc()
    nc = _prog

    f32 = lambda a: np.ascontiguousarray(np.asarray(a, dtype=np.float32))
    x = f32(inputs["L0_attn_x"])
    common = dict(
        pjW=f32(inputs["L0_attn_proj_W"]).T.copy(),
        fcW=f32(inputs["L0_attn_fconv_W"]).T.copy(),
        bcW=f32(inputs["L0_attn_bconv_W"]).T.copy(),
        dbW=_ext_dbw(f32(inputs["L0_attn_deltaBC_W"])),
        dtW=f32(inputs["L0_attn_dt_W"]).T.copy(),
        Alog=_pk_km(f32(inputs["L0_attn_A_log"])),
        Dv=_pk_vec(inputs["L0_attn_D"]),
        lnw=_pk_vec(inputs["L0_attn_ln_w"]), lnb=_pk_vec(inputs["L0_attn_ln_b"]),
        pjb=_pk_vec(inputs["L0_attn_proj_b"]), fcb=_pk_vec(inputs["L0_attn_fconv_b"]),
        bcb=_pk_vec(inputs["L0_attn_bconv_b"]), dtb=_pk_vec(inputs["L0_attn_dt_b"]),
        ctW=_pk_km(f32(inputs["L0_ffn_centroids"]).T.copy()),
        ebv=f32(inputs["L0_ffn_expert_biases"]),
        wsu=f32(inputs["L0_ffn_Ws_up"]), wsd=f32(inputs["L0_ffn_Ws_down"]),
        wup=f32(inputs["L0_ffn_W_up"]), wdn=f32(inputs["L0_ffn_W_down"]),
    )
    in_maps = []
    for c in range(8):
        b, h = c // 2, c % 2
        xo = np.ascontiguousarray(x[b, h * T:(h + 1) * T, :])
        xp = np.ascontiguousarray(x[b, 0:T, :]) if h == 1 else xo
        al = np.full((128, 1), float(h), np.float32)
        in_maps.append(dict(common, x_own=xo, x_pre=xp, alpha=al))

    res = run_bass_kernel_spmd(nc, in_maps, core_ids=list(range(8)))
    outp = np.empty((B, S, D), dtype=np.float32)
    for c in range(8):
        b, h = c // 2, c % 2
        outp[b, h * T:(h + 1) * T, :] = res.results[c]["out"]
    return outp
